# version 12 (final)
# baseline (speedup 1.0000x reference)
import numpy as np
import ml_dtypes
import concourse.bass as bass
import concourse.mybir as mybir
from concourse.bass_utils import run_bass_kernel_spmd
from contextlib import ExitStack

F32 = mybir.dt.float32
BF16 = mybir.dt.bfloat16
AF = mybir.ActivationFunctionType
ALU = mybir.AluOpType
AX = mybir.AxisListType.X

P = 128
D = 2048
H = 1024
KC = D // P
EPS = 1e-5
CH1 = 8
CH2 = 4
FRING = 10
ARING = 4
RG = [[0, 1], [2, 3], [4, 5], [6, 7]]


class Ctr:
    def __init__(self, h):
        self.h = h
        self.n = 0

    def inc(self, ins, by=1):
        ins.then_inc(self.h, by)
        self.n += by
        return (self, self.n)


class Waits:
    def __init__(self):
        self.last = {}

    def __call__(self, eng, tok):
        if tok is None:
            return
        ctr, val = tok
        if val is None or val <= 0:
            return
        key = (str(eng.engine), id(ctr))
        if self.last.get(key, 0) >= val:
            return
        eng.wait_ge(ctr.h, val)
        self.last[key] = val


def build_nc(SEQ, dbg=False):
    NT = SEQ // P
    NCH1 = NT // CH1
    NCH2 = NT // CH2
    RPC1 = CH1 * P
    RPC2 = CH2 * P

    nc = bass.Bass("TRN2", target_bir_lowering=False, debug=False)

    xT = nc.dram_tensor("xT", [D, SEQ], F32, kind="ExternalInput").ap()
    x_in = nc.dram_tensor("x", [SEQ, D], F32, kind="ExternalInput").ap()
    xh_in = nc.dram_tensor("xh", [SEQ, H], F32, kind="ExternalInput").ap()
    w_in = {nm: nc.dram_tensor(nm, [D, H], F32, kind="ExternalInput").ap()
            for nm in ("wq", "wk", "wv", "wg", "wf")}
    b_in = {nm: nc.dram_tensor(nm, [1, H], F32, kind="ExternalInput").ap()
            for nm in ("bq", "bk", "bv", "bg", "bf")}
    tri_in = nc.dram_tensor("tri", [P, P], F32, kind="ExternalInput").ap()
    id32_in = nc.dram_tensor("id32", [P, P], F32, kind="ExternalInput").ap()
    idbf_in = nc.dram_tensor("idbf", [P, P], BF16, kind="ExternalInput").ap()
    onc_in = nc.dram_tensor("onc", [P, 1], BF16, kind="ExternalInput").ap()
    onr_in = nc.dram_tensor("onr", [1, P], F32, kind="ExternalInput").ap()
    out_d = nc.dram_tensor("out", [SEQ, H], F32, kind="ExternalOutput").ap()

    asp = [nc.dram_tensor(f"asp{c}", [RPC1, H], F32) for c in range(NCH1)]
    agt = [nc.dram_tensor(f"agt{c}", [2 * RPC1, H], F32) for c in range(NCH1)]
    msp = [nc.dram_tensor(f"msp{c}", [RPC1, 4], F32) for c in range(NCH1)]
    mgt = [nc.dram_tensor(f"mgt{c}", [2 * RPC1, 4], F32) for c in range(NCH1)]
    mlc = nc.dram_tensor("mlc", [SEQ, 2], F32)
    m4sp = [nc.dram_tensor(f"m4sp{c}", [2 * RPC2, 1], F32) for c in range(NCH2)]
    m4gt = [nc.dram_tensor(f"m4gt{c}", [4 * RPC2, 1], F32) for c in range(NCH2)]

    W = Waits()

    with ExitStack() as top:
        E = top.enter_context
        s_tri = E(nc.sbuf_tensor("s_tri", [P, P], F32))
        s_id32 = E(nc.sbuf_tensor("s_id32", [P, P], F32))
        s_idbf = E(nc.sbuf_tensor("s_idbf", [P, P], BF16))
        s_onc = E(nc.sbuf_tensor("s_onc", [P, 1], BF16))
        s_onr = E(nc.sbuf_tensor("s_onr", [1, P], F32))
        s_eps = E(nc.sbuf_tensor("s_eps", [P, 1], F32))
        s_bb = E(nc.sbuf_tensor("s_bb", [1, H], BF16))
        s_ex = {nm: E(nc.sbuf_tensor(f"s_ex_{nm}", [2, H], BF16))
                for nm in ("q", "k", "v", "g", "f")}

        sV = Ctr(E(nc.semaphore("sV")))
        sS = Ctr(E(nc.semaphore("sS")))
        sT = Ctr(E(nc.semaphore("sT")))
        sG = Ctr(E(nc.semaphore("sG")))
        vb = Ctr(E(nc.semaphore("vb")))
        cca = Ctr(E(nc.semaphore("cca")))
        ccm = Ctr(E(nc.semaphore("ccm")))
        cc4 = Ctr(E(nc.semaphore("cc4")))
        dcn = Ctr(E(nc.semaphore("dcn")))
        dw = [Ctr(E(nc.semaphore(f"dw{b}"))) for b in range(2)]
        dxT = Ctr(E(nc.semaphore("dxT")))
        dx = Ctr(E(nc.semaphore("dx")))
        dxh = [Ctr(E(nc.semaphore(f"dxh{b}"))) for b in range(2)]
        dcy = Ctr(E(nc.semaphore("dcy")))
        dsp = [Ctr(E(nc.semaphore(f"dsp{b}"))) for b in range(ARING)]
        dmo = [Ctr(E(nc.semaphore(f"dmo{b}"))) for b in range(4)]
        dml = [Ctr(E(nc.semaphore(f"dml{b}"))) for b in range(2)]
        dx2 = [Ctr(E(nc.semaphore(f"dx2{b}"))) for b in range(2)]
        daf = [Ctr(E(nc.semaphore(f"daf{b}"))) for b in range(2)]
        dam = [Ctr(E(nc.semaphore(f"dam{b}"))) for b in range(2)]
        dm2 = [Ctr(E(nc.semaphore(f"dm2{b}"))) for b in range(2)]
        dml2 = [Ctr(E(nc.semaphore(f"dml2{b}"))) for b in range(2)]
        dmo4 = [Ctr(E(nc.semaphore(f"dmo4{b}"))) for b in range(4)]
        dmo4b = [Ctr(E(nc.semaphore(f"dmo4b{b}"))) for b in range(4)]
        dm4 = [Ctr(E(nc.semaphore(f"dm4{b}"))) for b in range(4)]
        dso = [Ctr(E(nc.semaphore(f"dso{b}"))) for b in range(4)]
        dal = Ctr(E(nc.semaphore("dal")))

        block = E(nc.Block())

        def dma(eng, ctr, dst, src):
            ins = eng.dma_start(out=dst, in_=src)
            return ctr.inc(ins, 16)

        def vt(v, ins):
            _, n = vb.inc(ins)
            v.wait_ge(vb.h, n)
            return (vb, n)

        with ExitStack() as ph1:
            E1 = ph1.enter_context
            s_w = {nm: E1(nc.sbuf_tensor(f"s_{nm}", [P, KC * H], BF16))
                   for nm in ("wq", "wk", "wv", "wg")}
            s_xt = E1(nc.sbuf_tensor("s_xt", [P, D], F32))
            s_xTf = E1(nc.sbuf_tensor("s_xTf", [P, D], F32))
            s_xTb = [E1(nc.sbuf_tensor(f"s_xTb{b}", [P, D], BF16)) for b in range(2)]
            s_xhp = [E1(nc.sbuf_tensor(f"s_xhp{b}", [P, H], F32)) for b in range(2)]
            s_ks = E1(nc.sbuf_tensor("s_ks", [P, H], F32))
            s_gs = E1(nc.sbuf_tensor("s_gs", [P, H], F32))
            s_kv = E1(nc.sbuf_tensor("s_kv", [P, H], F32))
            s_qg = s_ks
            s_at = [E1(nc.sbuf_tensor(f"s_at{b}", [P, H], F32)) for b in range(ARING)]
            s_mo = [E1(nc.sbuf_tensor(f"s_mo{b}", [P, 4], F32)) for b in range(4)]
            s_cy = E1(nc.sbuf_tensor("s_cy", [1, H], F32))
            s_csb = s_kv
            s_st = [E1(nc.sbuf_tensor(f"s_st{b}", [P, 8], F32)) for b in range(2)]
            s_mc = [E1(nc.sbuf_tensor(f"s_mc{b}", [P, 2], F32)) for b in range(2)]
            s_mr = [E1(nc.sbuf_tensor(f"s_mr{b}", [2, P], BF16)) for b in range(2)]
            ps = E1(nc.psum_tensor("ps", [P, 4096], F32))
            pv = ps[:, 0:H]
            pq = ps[:, H:2 * H]
            pk = ps[:, 2 * H:3 * H]
            pg = ps[:, 3 * H:4 * H]
            pcs = pk

            @block.sync
            def _(s):
                for src, dst in ((tri_in, s_tri[:, :]), (id32_in, s_id32[:, :]),
                                 (idbf_in, s_idbf[:, :]), (onc_in, s_onc[:, :]),
                                 (onr_in, s_onr[:, :])):
                    dma(s, dcn, dst, src)

            cast_tok = {}
            rowdma = {}
            bias_hold = [None]
            for bi, nm in enumerate(("bq", "bk", "bv", "bg")):
                @block.sync
                def _(s, nm=nm, bi=bi):
                    if bi > 0:
                        W(s, rowdma[bi - 1])
                        W(s, cast_tok[bi - 1])
                    bias_hold[0] = dma(s, dcn, s_xTf[0:1, 0:H], b_in[nm])

                @block.vector
                def _(v, nm=nm, bi=bi):
                    if bi == 0:
                        v.memset(s_eps[:, :], EPS)
                        for b in range(4):
                            v.memset(s_mo[b][:, :], 0.0)
                    W(v, bias_hold[0])
                    cast_tok[bi] = sV.inc(
                        v.tensor_copy(s_bb[0:1, :], s_xTf[0:1, 0:H]))

                @block.sync
                def _(s, nm=nm, bi=bi):
                    W(s, cast_tok[bi])
                    rowdma[bi] = dma(s, dcn, s_ex[nm[1]][1:2, :], s_bb[0:1, :])
            consts_all = (dcn, dcn.n)

            wcast_tok = {}
            wld_hold = [None]
            for idx in range(4 * KC):
                nm = ("wq", "wk", "wv", "wg")[idx // KC]
                k = idx % KC
                slot = idx % 2
                stg = s_xt[:, slot * H:(slot + 1) * H]

                @block.sync
                def _(s, nm=nm, k=k, stg=stg, idx=idx, slot=slot):
                    if idx >= 2:
                        W(s, wcast_tok[idx - 2])
                    wld_hold[0] = dma(s, dw[slot], stg,
                                      w_in[nm][k * P:(k + 1) * P, :])

                @block.vector
                def _(v, nm=nm, k=k, stg=stg, idx=idx):
                    W(v, wld_hold[0])
                    wcast_tok[idx] = sV.inc(
                        v.tensor_copy(s_w[nm][:, k * H:(k + 1) * H], stg))

            sw_ps = {"q": ps[0:1, H:2 * H], "k": ps[0:1, 2 * H:3 * H],
                     "v": ps[0:1, 0:H], "g": ps[0:1, 3 * H:4 * H]}
            sw_mm_tok = {}
            for wi, nm in enumerate(("wq", "wk", "wv", "wg")):
                @block.tensor
                def _(t, nm=nm, wi=wi):
                    W(t, wcast_tok[wi * KC + KC - 1])
                    W(t, consts_all)
                    ins = None
                    for k in range(KC):
                        for n in range(2):
                            ins = t.matmul(
                                sw_ps[nm[1]][:, n * 512:(n + 1) * 512], s_onc[:, :],
                                s_w[nm][:, k * H + n * 512:k * H + (n + 1) * 512],
                                start=(k == 0), stop=(k == KC - 1))
                    sw_mm_tok[nm[1]] = sT.inc(ins)

            setup_done = [None]

            @block.vector
            def _(v):
                tok = None
                for nm in ("q", "k", "v", "g"):
                    W(v, sw_mm_tok[nm])
                    tok = sV.inc(v.tensor_copy(s_ex[nm][0:1, :], sw_ps[nm]))
                setup_done[0] = tok
            setup_v_tok = setup_done[0]

            tokV, tokS, tokT, tokD = {}, {}, {}, {}

            def proj(t, xtb, wsb, pdst):
                for k in range(KC):
                    for n in range(2):
                        t.matmul(
                            pdst[:, n * 512:(n + 1) * 512],
                            xtb[:, k * P:(k + 1) * P],
                            wsb[:, k * H + n * 512:k * H + (n + 1) * 512],
                            start=(k == 0), stop=False)

            def proj_close(t, pdst, exname, mr):
                ins = None
                for n in range(2):
                    ins = t.matmul(pdst[:, n * 512:(n + 1) * 512], mr[:, :],
                                   s_ex[exname][:, n * 512:(n + 1) * 512],
                                   start=False, stop=True)
                return ins

            def emit_front(j):
                stj = s_st[j % 2]
                mcj = s_mc[j % 2]
                xtb = s_xTb[j % 2]

                @block.vector
                def _(v, j=j, stj=stj, xtb=xtb):
                    if j == 0:
                        W(v, setup_v_tok)
                    W(v, tokD[(j, "xT")])
                    if j >= 2:
                        W(v, tokT[(j - 2, "g")])
                    tokV[(j, "cast")] = sV.inc(v.tensor_copy(xtb[:, :], s_xTf[:, :]))
                    W(v, tokD[(j, "x")])
                    if j >= 2:
                        W(v, tokD[(j - 2, "ml")])
                    vt(v, v.reduce_sum(stj[:, 0:1], s_xt[:, :], axis=AX))
                    vt(v, v.tensor_scalar(stj[:, 2:3], stj[:, 0:1], 1.0 / D, None,
                                          ALU.mult))
                    vt(v, v.tensor_tensor(stj[:, 3:4], stj[:, 2:3], stj[:, 2:3],
                                          ALU.mult))
                    vt(v, v.scalar_tensor_tensor(s_xTf[:, :], s_xt[:, :], 1.0,
                                                 s_xt[:, :], ALU.mult, ALU.mult,
                                                 accum_out=stj[:, 1:2]))
                    tokV[(j, "var")] = sV.inc(
                        v.tensor_scalar(stj[:, 4:5], stj[:, 1:2], 1.0 / D,
                                        stj[:, 3:4], ALU.mult, ALU.subtract))

                @block.scalar
                def _(s, j=j, stj=stj, mcj=mcj):
                    W(s, tokV[(j, "var")])
                    if j >= 2:
                        W(s, tokT[(j - 2, "tr")])
                    tokS[(j, "std")] = sS.inc(
                        s.activation(mcj[:, 1:2], stj[:, 4:5], AF.Sqrt,
                                     bias=s_eps[:, :]))

                @block.vector
                def _(v, j=j, stj=stj, mcj=mcj):
                    W(v, tokS[(j, "std")])
                    vt(v, v.reciprocal(stj[:, 5:6], mcj[:, 1:2]))
                    vt(v, v.tensor_tensor(stj[:, 6:7], stj[:, 5:6], stj[:, 5:6],
                                          ALU.mult))
                    tokV[(j, "stat")] = sV.inc(
                        v.tensor_scalar(mcj[:, 0:1], stj[:, 2:3], -1.0, None,
                                        ALU.mult))

            @block.sync
            def _(s):
                W(s, rowdma[3])
                W(s, cast_tok[3])
                src = xT.rearrange("(k p) q -> p k q", p=P)[:, :, 0:P]
                dst = s_xTf[:, :].rearrange("p (k c) -> p k c", k=KC)
                tokD[(0, "xT")] = dma(s, dxT, dst, src)
                W(s, wcast_tok[4 * KC - 1])
                tokD[(0, "x")] = dma(s, dx, s_xt[:, :], x_in[0:P, :])
                tokD[(0, "xh")] = dma(s, dxh[0], s_xhp[0][:, :], xh_in[0:P, :])
            emit_front(0)

            for i in range(NT):
                xtb = s_xTb[i % 2]
                atb = s_at[i % ARING]
                mob = s_mo[i % 4]
                xhb = s_xhp[i % 2]
                sti = s_st[i % 2]
                mri = s_mr[i % 2]
                rstd = sti[:, 5:6]
                rstd2 = sti[:, 6:7]

                if i + 1 < NT:
                    @block.sync
                    def _(s, i=i):
                        j = i + 1
                        W(s, tokV[(i, "var")])
                        src = xT.rearrange("(k p) q -> p k q",
                                           p=P)[:, :, j * P:(j + 1) * P]
                        dst = s_xTf[:, :].rearrange("p (k c) -> p k c", k=KC)
                        tokD[(j, "xT")] = dma(s, dxT, dst, src)
                        W(s, tokV[(i, "var")])
                        tokD[(j, "x")] = dma(s, dx, s_xt[:, :],
                                             x_in[j * P:(j + 1) * P, :])
                        if j >= 2:
                            W(s, tokV[(j - 2, "cross")])
                        tokD[(j, "xh")] = dma(s, dxh[j % 2], s_xhp[j % 2][:, :],
                                              xh_in[j * P:(j + 1) * P, :])

                @block.tensor
                def _(t, i=i, xtb=xtb, mri=mri):
                    if i == 0:
                        W(t, setup_v_tok)
                        W(t, consts_all)
                    W(t, tokV[(i, "cast")])
                    if i > 0:
                        W(t, tokV[(i - 1, "kv")])
                    proj(t, xtb, s_w["wv"], pv)
                    W(t, tokV[(i, "stat")])
                    if i > 0:
                        W(t, tokS[(i - 1, "gs")])
                    ins = t.transpose(ps[0:2, 3 * H:3 * H + P], s_mc[i % 2][:, :],
                                      s_id32[:, :])
                    tokT[(i, "tr")] = sT.inc(ins)

                @block.vector
                def _(v, i=i, mri=mri):
                    W(v, tokT[(i, "tr")])
                    if i >= 2:
                        W(v, tokT[(i - 2, "g")])
                    tokV[(i, "mr")] = sV.inc(
                        v.tensor_copy(mri[:, :], ps[0:2, 3 * H:3 * H + P]))

                if i + 1 < NT:
                    emit_front(i + 1)

                @block.tensor
                def _(t, i=i, xtb=xtb, mri=mri):
                    if i > 0:
                        W(t, tokV[(i - 1, "qg")])
                    proj(t, xtb, s_w["wq"], pq)
                    W(t, tokV[(i, "mr")])
                    tokT[(i, "v")] = sT.inc(proj_close(t, pv, "v", mri))
                    tokT[(i, "q")] = sT.inc(proj_close(t, pq, "q", mri))
                    if i > 0:
                        W(t, tokV[(i - 1, "csb")])
                    proj(t, xtb, s_w["wk"], pk)
                    tokT[(i, "k")] = sT.inc(proj_close(t, pk, "k", mri))
                    proj(t, xtb, s_w["wg"], pg)
                    tokT[(i, "g")] = sT.inc(proj_close(t, pg, "g", mri))

                @block.scalar
                def _(s, i=i, rstd=rstd):
                    W(s, tokT[(i, "g")])
                    if i > 0:
                        W(s, tokV[(i - 1, "qg")])
                    tokS[(i, "gs")] = sS.inc(
                        s.activation(s_gs[:, :], pg[:, :], AF.Sigmoid, scale=rstd))

                @block.vector
                def _(v, i=i, rstd2=rstd2):
                    W(v, tokT[(i, "k")])
                    tokV[(i, "ks")] = vt(v, v.tensor_scalar(
                        s_ks[:, :], pk[:, :], rstd2, None, ALU.mult))
                    W(v, tokT[(i, "v")])
                    if i > 0:
                        W(v, tokT[(i - 1, "cs")])
                        W(v, tokD[(i - 1, "cy")])
                    tokV[(i, "kv")] = vt(v, v.tensor_tensor(
                        s_kv[:, :], s_ks[:, :], pv[:, :], ALU.mult))

                @block.tensor
                def _(t, i=i):
                    W(t, tokV[(i, "kv")])
                    W(t, tokV[(i, "ks")])
                    if i > 0:
                        W(t, tokD[(i - 1, "cy")])
                    ins = None
                    for n in range(2):
                        ins = t.matmul(pcs[:, n * 512:(n + 1) * 512], s_tri[:, :],
                                       s_kv[:, n * 512:(n + 1) * 512],
                                       start=True, stop=(i == 0))
                        if i > 0:
                            ins = t.matmul(pcs[:, n * 512:(n + 1) * 512],
                                           s_onr[:, :],
                                           s_cy[:, n * 512:(n + 1) * 512],
                                           start=False, stop=True)
                    tokT[(i, "cs")] = sT.inc(ins)

                @block.vector
                def _(v, i=i, atb=atb, mob=mob, xhb=xhb, rstd=rstd):
                    W(v, tokT[(i, "q")])
                    W(v, tokS[(i, "gs")])
                    tokV[(i, "qg")] = vt(v, v.scalar_tensor_tensor(
                        s_qg[:, :], pq[:, :], rstd, s_gs[:, :],
                        ALU.mult, ALU.mult))
                    W(v, tokT[(i, "cs")])
                    if i >= ARING:
                        W(v, tokD[(i - ARING, "at")])
                    tokV[(i, "attn")] = vt(v, v.tensor_tensor(
                        atb[:, :], s_qg[:, :], pcs[:, :], ALU.mult))
                    if i >= 4:
                        W(v, tokD[(i - 4, "mo")])
                    sV.inc(v.reduce_sum(mob[:, 0:1], atb[:, :], axis=AX))
                    W(v, tokD[(i, "xh")])
                    tokV[(i, "cross")] = vt(v, v.scalar_tensor_tensor(
                        s_kv[:, :], xhb[:, :], 1.0, atb[:, :], ALU.mult, ALU.mult,
                        accum_out=mob[:, 2:3]))
                    tokV[(i, "asq")] = vt(v, v.scalar_tensor_tensor(
                        s_kv[:, :], atb[:, :], 1.0, atb[:, :], ALU.mult, ALU.mult,
                        accum_out=mob[:, 1:2]))
                    if i > 0:
                        W(v, tokD[(i - 1, "cy")])
                    tokV[(i, "csb")] = vt(v, v.tensor_copy(
                        s_csb[:, :], pcs[:, :]))

                @block.gpsimd
                def _(g, i=i, atb=atb, mob=mob, sti=sti):
                    c = i // CH1
                    r = (i % CH1) * P
                    W(g, tokV[(i, "csb")])
                    tokD[(i, "cy")] = dma(g, dcy, s_cy[:, :], s_csb[P - 1:P, :])
                    W(g, tokV[(i, "attn")])
                    tokD[(i, "at")] = dma(g, dsp[i % ARING],
                                          asp[c].ap()[r:r + P, :], atb[:, :])
                    W(g, tokV[(i, "asq")])
                    tokD[(i, "mo")] = dma(g, dmo[i % 4],
                                          msp[c].ap()[r:r + P, :], mob[:, :])
                    tokD[(i, "ml")] = dma(g, dml[i % 2],
                                          mlc.ap()[i * P:(i + 1) * P, :],
                                          sti[:, 0:2])
                    if i % CH1 == CH1 - 1:
                        for b in range(ARING):
                            W(g, tokD[(i - ((i - b) % ARING), "at")])
                        for b in range(4):
                            W(g, tokD[(i - ((i - b) % 4), "mo")])
                        cca.inc(g.collective_compute(
                            "AllGather", ALU.bypass, replica_groups=RG,
                            ins=[asp[c].ap().opt()], outs=[agt[c].ap().opt()]))
                        ccm.inc(g.collective_compute(
                            "AllGather", ALU.bypass, replica_groups=RG,
                            ins=[msp[c].ap().opt()], outs=[mgt[c].ap().opt()]))

            ph1_end = dict(v=(sV, sV.n), s=(sS, sS.n), t=(sT, sT.n))
            ph1_mltok = {i: tokD[(i, "ml")] for i in range(NT)}
            ph1_dmas = [(c, c.n) for c in
                        ([dcy, dxT, dx, dcn] + dw + dxh + dsp + dmo + dml)]

        with ExitStack() as ph2:
            E2 = ph2.enter_context
            s_wf = E2(nc.sbuf_tensor("s_wf", [P, KC * H], BF16))
            s_x2 = [E2(nc.sbuf_tensor(f"s_x2_{b}", [P, D], F32)) for b in range(1)]
            s_af = [E2(nc.sbuf_tensor(f"s_af{b}", [P, D], F32)) for b in range(1)]
            s_am = [E2(nc.sbuf_tensor(f"s_am{b}", [P, H], F32)) for b in range(2)]
            s_xh = [E2(nc.sbuf_tensor(f"s_xh{b}", [P, H], F32)) for b in range(1)]
            s_l2 = E2(nc.sbuf_tensor("s_l2", [P, D], F32))
            s_l2m = E2(nc.sbuf_tensor("s_l2m", [P, H], F32))
            s_yb = [E2(nc.sbuf_tensor(f"s_yb{b}", [P, D], BF16)) for b in range(2)]
            s_yT = [E2(nc.sbuf_tensor(f"s_yT{b}", [P, D], BF16)) for b in range(2)]
            s_fr = [E2(nc.sbuf_tensor(f"s_fr{b}", [P, H], F32))
                    for b in range(FRING)]
            s_s1 = [E2(nc.sbuf_tensor(f"s_s1_{b}", [P, H], F32))
                    for b in range(FRING)]
            s_ot = [E2(nc.sbuf_tensor(f"s_ot{b}", [P, H], F32)) for b in range(4)]
            s_m2 = [E2(nc.sbuf_tensor(f"s_m2_{b}", [P, 8], F32)) for b in range(2)]
            s_mlb = [E2(nc.sbuf_tensor(f"s_mlb{b}", [P, 2], F32)) for b in range(2)]
            s_m4 = [E2(nc.sbuf_tensor(f"s_m4_{b}", [P, 12], F32)) for b in range(4)]
            s_mo4a = [E2(nc.sbuf_tensor(f"s_mo4a{b}", [P, 1], F32)) for b in range(4)]
            s_mo4b = [E2(nc.sbuf_tensor(f"s_mo4b{b}", [P, 1], F32)) for b in range(4)]
            s_st2 = [E2(nc.sbuf_tensor(f"s_st2_{b}", [P, 24], F32))
                     for b in range(2)]
            s_mc2 = [E2(nc.sbuf_tensor(f"s_mc2_{b}", [P, 2], F32))
                     for b in range(2)]
            s_mcb = [E2(nc.sbuf_tensor(f"s_mcb{b}", [P, 2], BF16))
                     for b in range(2)]
            s_mr2 = [E2(nc.sbuf_tensor(f"s_mr2{i}", [2, P], BF16)) for i in range(2)]
            s_dmp = E2(nc.sbuf_tensor("s_dmp", [P, H], F32))
            s_dsc = E2(nc.sbuf_tensor("s_dsc", [P, 128], F32))
            s_stg = [s_x2[0][:, 0:H], s_x2[0][:, H:D]]
            s_abs = [s_yT[0][:, 0:H], s_yT[0][:, H:D]]
            s_alr = s_l2[0:1, 0:H]
            s_bfs = s_alr
            s_alb = s_yb[0][0:1, 0:H]
            s_alf = s_yb[0][:, H:D]
            s_ssr = s_l2m[0:1, 0:H]
            ps2 = E2(nc.psum_tensor("ps2", [P, 2048], F32))
            psb = E2(nc.psum_tensor("psb", [P, 2176], BF16))

            def bwait(eng):
                W(eng, ph1_end["t"])
                W(eng, ph1_end["v"])
                W(eng, ph1_end["s"])
                for tok in ph1_dmas:
                    W(eng, tok)

            wfl_hold = [None]
            abs_tok, sgn_tok = {}, {}
            for k in range(KC):
                slot = k % 2

                @block.sync
                def _(s, k=k, slot=slot):
                    bwait(s)
                    if k >= 2:
                        W(s, sgn_tok[k - 2])
                    wfl_hold[0] = dma(s, dw[slot], s_stg[slot],
                                      w_in["wf"][k * P:(k + 1) * P, :])

                @block.scalar
                def _(s, k=k, slot=slot):
                    bwait(s)
                    W(s, wfl_hold[0])
                    if k >= 2:
                        W(s, abs_tok[k - 2])
                    sS.inc(s.activation(s_wf[:, k * H:(k + 1) * H], s_stg[slot],
                                        AF.Sign))
                    sgn_tok[k] = sS.inc(
                        s.activation(s_abs[slot][:, :], s_stg[slot], AF.Abs))

                @block.tensor
                def _(t, k=k, slot=slot):
                    bwait(t)
                    W(t, sgn_tok[k])
                    ins = None
                    for n in range(2):
                        ins = t.matmul(ps2[0:1, n * 512:(n + 1) * 512], s_onc[:, :],
                                       s_abs[slot][:, n * 512:(n + 1) * 512],
                                       start=(k == 0), stop=(k == KC - 1))
                    abs_tok[k] = sT.inc(ins)

            ss_tok = [None]

            @block.tensor
            def _(t):
                W(t, sgn_tok[KC - 1])
                ins = None
                for k in range(KC):
                    for n in range(2):
                        ins = t.matmul(ps2[0:1, H + n * 512:H + (n + 1) * 512],
                                       s_onc[:, :],
                                       s_wf[:, k * H + n * 512:k * H + (n + 1) * 512],
                                       start=(k == 0), stop=(k == KC - 1))
                ss_tok[0] = sT.inc(ins)

            alb_tok = [None]

            @block.vector
            def _(v):
                bwait(v)
                W(v, abs_tok[KC - 1])
                vt(v, v.tensor_scalar(s_alr, ps2[0:1, 0:H], 1.0 / D, None, ALU.mult))
                alb_tok[0] = sV.inc(v.tensor_copy(s_alb, s_alr))

            alf_tok = [None]

            @block.gpsimd
            def _(g):
                bwait(g)
                W(g, alb_tok[0])
                tok = dma(g, dal, s_alf[0:1, :], s_alb)
                rows = 1
                while rows < P:
                    W(g, tok)
                    tok = dma(g, dal, s_alf[rows:2 * rows, :], s_alf[0:rows, :])
                    rows *= 2
                alf_tok[0] = tok

            wfa_tok = {}
            for k in range(KC):
                @block.vector
                def _(v, k=k):
                    W(v, alf_tok[0])
                    W(v, ss_tok[0])
                    wfa_tok[k] = sV.inc(
                        v.tensor_tensor(s_wf[:, k * H:(k + 1) * H],
                                        s_wf[:, k * H:(k + 1) * H],
                                        s_alf[:, :], ALU.mult))

            sbwrow_tok = [None]

            @block.vector
            def _(v):
                W(v, ss_tok[0])
                vt(v, v.tensor_copy(s_ssr, ps2[0:1, H:2 * H]))
                vt(v, v.tensor_tensor(s_ssr, s_ssr, s_alr, ALU.mult))
                sbwrow_tok[0] = sV.inc(v.tensor_copy(s_ex["f"][0:1, :], s_ssr))

            bfl = [None]

            @block.sync
            def _(s):
                W(s, sbwrow_tok[0])
                bfl[0] = dma(s, dcn, s_bfs, b_in["bf"])

            exf_cast = [None]

            @block.vector
            def _(v):
                W(v, bfl[0])
                exf_cast[0] = sV.inc(v.tensor_copy(s_bb[0:1, :], s_bfs))

            exf_row = [None]

            @block.sync
            def _(s):
                W(s, exf_cast[0])
                exf_row[0] = dma(s, dcn, s_ex["f"][1:2, :], s_bb[0:1, :])

            tV, tS, tT, tD, tG = {}, {}, {}, {}, {}

            def emit_stats(j):
                c1 = j // CH1
                r1 = (j % CH1) * P
                c2 = j // CH2
                x2b = s_x2[0]
                afb = s_af[0]
                amb = s_am[j % 2]
                xhb = s_xh[0]
                m2b = s_m2[j % 2]
                mlb = s_mlb[j % 2]
                frb = s_fr[j % FRING]
                s1b = s_s1[j % FRING]
                moa = s_mo4a[j % 4]
                mosb = s_mo4b[j % 4]
                pf = ps2[:, (j % 2) * H:(j % 2 + 1) * H]
                st = s_st2[j % 2]
                mc2 = s_mc2[j % 2]
                mcb = s_mcb[j % 2]

                @block.sync
                def _(s, j=j, c1=c1, r1=r1, x2b=x2b, afb=afb, amb=amb, xhb=xhb,
                      m2b=m2b, mlb=mlb):
                    if j < 2:
                        W(s, sgn_tok[KC - 1])
                        W(s, exf_row[0])
                    if j >= 1:
                        W(s, tG[(j - 1, "y")])
                    tD[(j, "x")] = dma(s, dx2[0], x2b[:, :],
                                       x_in[j * P:(j + 1) * P, :])
                    W(s, (cca, c1 + 1))
                    if j >= 1:
                        W(s, tV[(j - 1, "l2")])
                    dma(s, daf[0], afb[:, 0:H], agt[c1].ap()[r1:r1 + P, :])
                    tD[(j, "af")] = dma(s, daf[0], afb[:, H:D],
                                        agt[c1].ap()[RPC1 + r1:RPC1 + r1 + P, :])


                @block.scalar
                def _(s, j=j, c1=c1, r1=r1, m2b=m2b, mlb=mlb):
                    if j == 0:
                        bwait(s)
                    W(s, (ccm, c1 + 1))
                    if j >= 2:
                        W(s, tV[(j - 2, "m2c")])
                    dma(s, dm2[j % 2], m2b[:, 0:4], mgt[c1].ap()[r1:r1 + P, :])
                    tD[(j, "m2")] = dma(s, dm2[j % 2], m2b[:, 4:8],
                                        mgt[c1].ap()[RPC1 + r1:RPC1 + r1 + P, :])
                    if j >= 2:
                        W(s, tV[(j - 2, "alg")])
                    W(s, ph1_mltok[j])
                    tD[(j, "ml")] = dma(s, dml2[j % 2], mlb[:, :],
                                        mlc.ap()[j * P:(j + 1) * P, :])

                @block.vector
                def _(v, j=j, m2b=m2b, st=st):
                    if j == 0:
                        bwait(v)
                    W(v, tD[(j, "m2")])
                    vt(v, v.tensor_tensor(st[:, 0:4], m2b[:, 0:4], m2b[:, 4:8],
                                          ALU.add))
                    vt(v, v.tensor_scalar(st[:, 4:5], st[:, 0:1], 1.0 / D, None,
                                          ALU.mult))
                    vt(v, v.tensor_tensor(st[:, 5:6], st[:, 4:5], st[:, 4:5],
                                          ALU.mult))
                    tV[(j, "m2c")] = sV.inc(
                        v.tensor_scalar(st[:, 6:7], st[:, 1:2], 1.0 / D,
                                        st[:, 5:6], ALU.mult, ALU.subtract))

                @block.scalar
                def _(s, j=j, st=st, c1=c1, r1=r1, amb=amb):
                    if j == 0:
                        bwait(s)
                    s.activation(s_dsc[:, (2 * j) % 128:(2 * j) % 128 + 1],
                                 s_eps[:, :], AF.Sqrt, bias=s_eps[:, :])
                    W(s, tV[(j, "m2c")])
                    tS[(j, "std2")] = sS.inc(
                        s.activation(st[:, 7:8], st[:, 6:7], AF.Sqrt,
                                     bias=s_eps[:, :]))
                    if j >= 2:
                        W(s, tV[(j - 2, "l2m")])
                    tD[(j, "am")] = dma(s, dam[j % 2], amb[:, :],
                                        asp[c1].ap()[r1:r1 + P, :])

                @block.vector
                def _(v, j=j, st=st, afb=afb, amb=amb):
                    W(v, tS[(j, "std2")])
                    vt(v, v.reciprocal(st[:, 8:9], st[:, 7:8]))
                    tV[(j, "r2")] = vt(v, v.tensor_scalar(
                        st[:, 9:10], st[:, 4:5], st[:, 8:9], -1.0,
                        ALU.mult, ALU.mult))
                    W(v, tD[(j, "af")])
                    if j >= 1:
                        W(v, tG[(j - 1, "y")])
                    tV[(j, "l2a")] = sV.inc(
                        v.tensor_scalar(s_l2[:, 0:H], afb[:, 0:H], st[:, 8:9],
                                        st[:, 9:10], ALU.mult, ALU.add))
                    tV[(j, "l2")] = sV.inc(
                        v.tensor_scalar(s_l2[:, H:D], afb[:, H:D], st[:, 8:9],
                                        st[:, 9:10], ALU.mult, ALU.add))
                    W(v, tD[(j, "am")])
                    if j >= 1:
                        W(v, tG[(j - 1, "s1")])
                    tV[(j, "l2m")] = sV.inc(
                        v.tensor_scalar(s_l2m[:, :], amb[:, :], st[:, 8:9],
                                        st[:, 9:10], ALU.mult, ALU.add))

                @block.vector
                def _(v, j=j, st=st, mlb=mlb):
                    W(v, tD[(j, "ml")])
                    W(v, tV[(j, "r2")])
                    v.tensor_scalar(st[:, 10:11], mlb[:, 0:1], 1.0 / D, None,
                                    ALU.mult)
                    v.tensor_scalar(st[:, 11:12], mlb[:, 0:1], st[:, 4:5],
                                    st[:, 2:3], ALU.mult, ALU.subtract)
                    v.tensor_scalar(st[:, 13:14], st[:, 5:6], float(D), None,
                                    ALU.mult)
                    vt(v, v.tensor_tensor(st[:, 15:16], st[:, 8:9], st[:, 8:9],
                                          ALU.mult))
                    v.tensor_scalar(st[:, 12:13], st[:, 11:12], st[:, 8:9],
                                    -2.0, ALU.mult, ALU.mult)
                    v.tensor_scalar(st[:, 14:15], st[:, 1:2], 1.0,
                                    st[:, 13:14], ALU.mult, ALU.subtract)
                    vt(v, v.tensor_tensor(st[:, 18:19], st[:, 10:11], st[:, 10:11],
                                          ALU.mult))
                    v.tensor_tensor(st[:, 16:17], st[:, 14:15], st[:, 15:16],
                                    ALU.mult)
                    vt(v, v.tensor_tensor(st[:, 17:18], mlb[:, 1:2], st[:, 12:13],
                                          ALU.add))
                    vt(v, v.tensor_tensor(st[:, 17:18], st[:, 17:18], st[:, 16:17],
                                          ALU.add))
                    tV[(j, "alg")] = sV.inc(
                        v.tensor_scalar(st[:, 19:20], st[:, 17:18], 1.0 / D,
                                        st[:, 18:19], ALU.mult, ALU.subtract))

                @block.scalar
                def _(s, j=j, st=st, xhb=xhb):
                    s.activation(s_dsc[:, (2 * j + 1) % 128:(2 * j + 1) % 128 + 1],
                                 s_eps[:, :], AF.Sqrt, bias=s_eps[:, :])
                    W(s, tV[(j, "alg")])
                    tS[(j, "std3")] = sS.inc(
                        s.activation(st[:, 20:21], st[:, 19:20], AF.Sqrt,
                                     bias=s_eps[:, :]))
                    if j >= 1:
                        W(s, tG[(j - 1, "s1")])
                    tD[(j, "xh")] = dma(s, dxh[0], xhb[:, :],
                                        xh_in[j * P:(j + 1) * P, :])

                @block.vector
                def _(v, j=j, st=st):
                    W(v, tS[(j, "std3")])
                    vt(v, v.reciprocal(st[:, 21:22], st[:, 20:21]))
                    vt(v, v.tensor_scalar(st[:, 22:23], st[:, 10:11], -1.0, None,
                                          ALU.mult))
                    vt(v, v.tensor_copy(mc2[:, 0:1], st[:, 22:23]))
                    vt(v, v.tensor_copy(mc2[:, 1:2], st[:, 20:21]))
                    if j >= 2:
                        W(v, tT[(j - 2, "trm")])
                    tV[(j, "mcb")] = sV.inc(v.tensor_copy(mcb[:, :], mc2[:, :]))

                @block.gpsimd
                def _(g, j=j, x2b=x2b, s1b=s1b, xhb=xhb):
                    if j == 0:
                        bwait(g)
                        W(g, sbwrow_tok[0])
                    W(g, tV[(j, "l2a")])
                    W(g, tD[(j, "x")])
                    if j >= 2:
                        W(g, tT[(j - 2, "tr")])
                    if j <= 1:
                        W(g, alf_tok[0])
                    tG[(j, "ya")] = sG.inc(
                        g.tensor_tensor(s_yb[j % 2][:, 0:H], x2b[:, 0:H],
                                        s_l2[:, 0:H], ALU.add))
                    W(g, tV[(j, "l2")])
                    tG[(j, "y")] = sG.inc(
                        g.tensor_tensor(s_yb[j % 2][:, H:D], x2b[:, H:D],
                                        s_l2[:, H:D], ALU.add))
                    W(g, tV[(j, "l2m")])
                    W(g, tD[(j, "xh")])
                    if j >= FRING:
                        W(g, tV[(j - FRING, "ot")])
                    tG[(j, "s1")] = sG.inc(
                        g.tensor_tensor(s1b[:, :], xhb[:, :], s_l2m[:, :], ALU.add))


            def emit_workA(j):
                c1 = j // CH1
                r1 = (j % CH1) * P
                c2 = j // CH2
                x2b = s_x2[0]
                afb = s_af[0]
                amb = s_am[j % 2]
                xhb = s_xh[0]
                m2b = s_m2[j % 2]
                mlb = s_mlb[j % 2]
                frb = s_fr[j % FRING]
                s1b = s_s1[j % FRING]
                moa = s_mo4a[j % 4]
                mosb = s_mo4b[j % 4]
                pf = ps2[:, (j % 2) * H:(j % 2 + 1) * H]
                st = s_st2[j % 2]
                mc2 = s_mc2[j % 2]
                mcb = s_mcb[j % 2]

                @block.tensor
                def _(t, j=j):
                    W(t, tG[(j, "ya")])
                    if j == 0:
                        W(t, ss_tok[0])
                    if j >= 1:
                        W(t, tV[(j - 1, "yTea")])
                    ins = None
                    for k in range(KC // 2):
                        ins = t.transpose(psb[:, k * P:(k + 1) * P],
                                          s_yb[j % 2][:, k * P:(k + 1) * P],
                                          s_idbf[:, :])
                    tT[(j, "tra")] = sT.inc(ins)
                    W(t, tG[(j, "y")])
                    if j >= 1:
                        W(t, tV[(j - 1, "yTe")])
                    for k in range(KC // 2, KC):
                        ins = t.transpose(psb[:, k * P:(k + 1) * P],
                                          s_yb[j % 2][:, k * P:(k + 1) * P],
                                          s_idbf[:, :])
                    tT[(j, "tr")] = sT.inc(ins)
                    W(t, tV[(j, "mcb")])
                    ins = t.transpose(psb[0:2, 2048:2176], s_mcb[j % 2][:, :],
                                      s_idbf[:, :])
                    tT[(j, "trm")] = sT.inc(ins)

                @block.vector
                def _(v, j=j):
                    W(v, tT[(j, "tra")])
                    if j <= 1:
                        for k in range(KC):
                            W(v, wfa_tok[k])
                        W(v, ss_tok[0])
                    if j >= 2:
                        W(v, tT[(j - 2, "f")])
                    tV[(j, "yTea")] = sV.inc(
                        v.tensor_copy(s_yT[j % 2][:, 0:H], psb[:, 0:H]))
                    W(v, tT[(j, "tr")])
                    tV[(j, "yTe")] = sV.inc(
                        v.tensor_copy(s_yT[j % 2][:, H:D], psb[:, H:D]))
                    W(v, tT[(j, "trm")])
                    tV[(j, "mr2")] = sV.inc(
                        v.tensor_copy(s_mr2[j % 2][:, :], psb[0:2, 2048:2176]))

            def emit_workB(j):
                c1 = j // CH1
                r1 = (j % CH1) * P
                c2 = j // CH2
                x2b = s_x2[0]
                afb = s_af[0]
                amb = s_am[j % 2]
                xhb = s_xh[0]
                m2b = s_m2[j % 2]
                mlb = s_mlb[j % 2]
                frb = s_fr[j % FRING]
                s1b = s_s1[j % FRING]
                moa = s_mo4a[j % 4]
                mosb = s_mo4b[j % 4]
                pf = ps2[:, (j % 2) * H:(j % 2 + 1) * H]
                st = s_st2[j % 2]
                mc2 = s_mc2[j % 2]
                mcb = s_mcb[j % 2]

                @block.tensor
                def _(t, j=j, pf=pf):
                    W(t, tV[(j, "yTe")])
                    for k in range(KC):
                        W(t, wfa_tok[k])
                    W(t, exf_row[0])
                    if j >= 2:
                        W(t, tV[(j - 2, "fev")])
                    for k in range(KC):
                        for n in range(2):
                            t.matmul(
                                pf[:, n * 512:(n + 1) * 512],
                                s_yT[j % 2][:, k * P:(k + 1) * P],
                                s_wf[:, k * H + n * 512:k * H + (n + 1) * 512],
                                start=(k == 0), stop=False)
                    W(t, tV[(j, "mr2")])
                    ins = None
                    for n in range(2):
                        ins = t.matmul(pf[:, n * 512:(n + 1) * 512], s_mr2[j % 2][:, :],
                                       s_ex["f"][:, n * 512:(n + 1) * 512],
                                       start=False, stop=True)
                    tT[(j, "f")] = sT.inc(ins)

                @block.vector
                def _(v, j=j, frb=frb, pf=pf, st=st, mosb=mosb, moa=moa):
                    W(v, tT[(j, "f")])
                    if j >= FRING:
                        W(v, tV[(j - FRING, "ot")])
                    tV[(j, "fev")] = vt(v, v.tensor_scalar(
                        frb[:, :], pf[:, :], st[:, 21:22], None, ALU.mult))
                    if j >= 4:
                        W(v, tD[(j - 4, "mo4a")])
                    tV[(j, "fred")] = sV.inc(
                        v.reduce_sum(moa[:, 0:1], frb[:, :], axis=AX))
                    if j >= 4:
                        W(v, tD[(j - 4, "mo4b")])
                    tV[(j, "fsq")] = sV.inc(
                        v.scalar_tensor_tensor(s_dmp[:, :], frb[:, :], 1.0,
                                               frb[:, :], ALU.mult, ALU.mult,
                                               accum_out=mosb[:, 0:1]))

                @block.gpsimd
                def _(g, j=j, frb=frb, moa=moa, xhb=xhb, s1b=s1b, c2=c2,
                      mosb=mosb):
                    r2 = (j % CH2) * P
                    W(g, tV[(j, "fred")])
                    tD[(j, "mo4a")] = dma(g, dmo4[j % 4],
                                          m4sp[c2].ap()[r2:r2 + P, :], moa[:, :])
                    W(g, tV[(j, "fsq")])
                    tD[(j, "mo4b")] = dma(g, dmo4b[j % 4],
                                          m4sp[c2].ap()[RPC2 + r2:RPC2 + r2 + P, :],
                                          mosb[:, :])
                    if j % CH2 == CH2 - 1:
                        for b in range(4):
                            W(g, tD[(j - ((j - b) % 4), "mo4b")])
                            W(g, tD[(j - ((j - b) % 4), "mo4a")])
                        cc4.inc(g.collective_compute(
                            "AllGather", ALU.bypass, replica_groups=RG,
                            ins=[m4sp[c2].ap().opt()], outs=[m4gt[c2].ap().opt()]))

            def emit_finalize_chunk(c2):
                jfs = list(range(c2 * CH2, (c2 + 1) * CH2))

                @block.gpsimd
                def _(g, c2=c2, jfs=jfs):
                    W(g, (cc4, c2 + 1))
                    for jf in jfs:
                        r2 = (jf % CH2) * P
                        m4b = s_m4[jf % 4]
                        if jf >= 4:
                            W(g, tV[(jf - 4, "r4")])
                        dma(g, dm4[jf % 4], m4b[:, 0:1],
                            m4gt[c2].ap()[r2:r2 + P, :])
                        dma(g, dm4[jf % 4], m4b[:, 1:2],
                            m4gt[c2].ap()[RPC2 + r2:RPC2 + r2 + P, :])
                        dma(g, dm4[jf % 4], m4b[:, 2:3],
                            m4gt[c2].ap()[2 * RPC2 + r2:2 * RPC2 + r2 + P, :])
                        tD[(jf, "m4")] = dma(
                            g, dm4[jf % 4], m4b[:, 3:4],
                            m4gt[c2].ap()[3 * RPC2 + r2:3 * RPC2 + r2 + P, :])

                @block.vector
                def _(v, jfs=jfs):
                    for jf in jfs:
                        m4b = s_m4[jf % 4]
                        W(v, tD[(jf, "m4")])
                        vt(v, v.tensor_tensor(m4b[:, 4:6], m4b[:, 0:2],
                                              m4b[:, 2:4], ALU.add))
                        vt(v, v.tensor_scalar(m4b[:, 6:7], m4b[:, 4:5], 1.0 / D,
                                              None, ALU.mult))
                        vt(v, v.tensor_tensor(m4b[:, 7:8], m4b[:, 6:7],
                                              m4b[:, 6:7], ALU.mult))
                        tV[(jf, "var4")] = sV.inc(
                            v.tensor_scalar(m4b[:, 8:9], m4b[:, 5:6], 1.0 / D,
                                            m4b[:, 7:8], ALU.mult, ALU.subtract))

                @block.scalar
                def _(s, jfs=jfs):
                    for jf in jfs:
                        m4b = s_m4[jf % 4]
                        W(s, tV[(jf, "var4")])
                        tS[(jf, "std4")] = sS.inc(
                            s.activation(m4b[:, 11:12], m4b[:, 8:9], AF.Sqrt,
                                         bias=s_eps[:, :]))

                @block.vector
                def _(v, jfs=jfs):
                    for jf in jfs:
                        m4b = s_m4[jf % 4]
                        frb = s_fr[jf % FRING]
                        s1b = s_s1[jf % FRING]
                        otb = s_ot[jf % 4]
                        W(v, tS[(jf, "std4")])
                        vt(v, v.reciprocal(m4b[:, 9:10], m4b[:, 11:12]))
                        tV[(jf, "r4")] = vt(v, v.tensor_scalar(
                            m4b[:, 10:11], m4b[:, 6:7], m4b[:, 9:10], -1.0,
                            ALU.mult, ALU.mult))
                        vt(v, v.scalar_tensor_tensor(
                            s1b[:, :], frb[:, :], m4b[:, 9:10], s1b[:, :],
                            ALU.mult, ALU.add))
                        if jf >= 4:
                            W(v, tD[(jf - 4, "od")])
                        tV[(jf, "ot")] = vt(v, v.tensor_scalar(
                            otb[:, :], s1b[:, :], m4b[:, 10:11], None, ALU.add))

                @block.gpsimd
                def _(g, jfs=jfs):
                    for jf in jfs:
                        otb = s_ot[jf % 4]
                        W(g, tV[(jf, "ot")])
                        tD[(jf, "od")] = dma(g, dso[jf % 4],
                                             out_d[jf * P:(jf + 1) * P, :],
                                             otb[:, :])

            emit_stats(0)
            for j in range(NT):
                if j % CH2 == 0 and j // CH2 >= 2:
                    emit_finalize_chunk(j // CH2 - 2)
                emit_workA(j)
                if j + 1 < NT:
                    emit_stats(j + 1)
                if j >= 1:
                    emit_workB(j - 1)
            emit_workB(NT - 1)
            emit_finalize_chunk(NCH2 - 2)
            emit_finalize_chunk(NCH2 - 1)

            @block.gpsimd
            def _(g):
                for b in range(4):
                    g.wait_ge(dso[b].h, dso[b].n)

    return nc


def prep_inputs(xb, Wq, bq, Wk, bk, Wv, bv, Wg, bg, Wf, bf, pi):
    f32 = np.float32
    lo, hi = pi * H, (pi + 1) * H
    return {
        "xT": np.ascontiguousarray(xb.T).astype(f32, copy=False),
        "x": np.ascontiguousarray(xb),
        "xh": np.ascontiguousarray(xb[:, lo:hi]),
        "wq": np.ascontiguousarray(Wq.T[:, lo:hi]),
        "wk": np.ascontiguousarray(Wk.T[:, lo:hi]),
        "wv": np.ascontiguousarray(Wv.T[:, lo:hi]),
        "wg": np.ascontiguousarray(Wg.T[:, lo:hi]),
        "wf": np.ascontiguousarray(Wf.T[:, lo:hi]),
        "bq": np.ascontiguousarray(bq[lo:hi]).reshape(1, H),
        "bk": np.ascontiguousarray(bk[lo:hi]).reshape(1, H),
        "bv": np.ascontiguousarray(bv[lo:hi]).reshape(1, H),
        "bg": np.ascontiguousarray(bg[lo:hi]).reshape(1, H),
        "bf": np.ascontiguousarray(bf[lo:hi]).reshape(1, H),
        "tri": np.triu(np.ones((P, P), dtype=f32)),
        "id32": np.eye(P, dtype=f32),
        "idbf": np.eye(P, dtype=ml_dtypes.bfloat16),
        "onc": np.ones((P, 1), dtype=ml_dtypes.bfloat16),
        "onr": np.ones((1, P), dtype=f32),
    }


def kernel(L0_attn_x, L0_attn_Wq, L0_attn_bq, L0_attn_Wk, L0_attn_bk,
           L0_attn_Wv, L0_attn_bv, L0_attn_Wg, L0_attn_bg,
           L0_ffn_weight, L0_ffn_bias):
    x = np.asarray(L0_attn_x, dtype=np.float32)
    B, SEQ, Dm = x.shape
    assert Dm == D and B * 2 == 8
    nc = build_nc(SEQ)
    in_maps = []
    for c in range(8):
        p, pi = c // 2, c % 2
        in_maps.append(prep_inputs(
            x[p], np.asarray(L0_attn_Wq), np.asarray(L0_attn_bq),
            np.asarray(L0_attn_Wk), np.asarray(L0_attn_bk),
            np.asarray(L0_attn_Wv), np.asarray(L0_attn_bv),
            np.asarray(L0_attn_Wg), np.asarray(L0_attn_bg),
            np.asarray(L0_ffn_weight), np.asarray(L0_ffn_bias), pi))
    res = run_bass_kernel_spmd(nc, in_maps, core_ids=list(range(8)))
    out = np.empty((B, SEQ, D), dtype=np.float32)
    for c in range(8):
        p, pi = c // 2, c % 2
        out[p][:, pi * H:(pi + 1) * H] = res.results[c]["out"]
    return out


# version 13
# speedup vs baseline: 1.0252x; 1.0252x over previous
import numpy as np
import ml_dtypes
import concourse.bass as bass
import concourse.mybir as mybir
from concourse.bass_utils import run_bass_kernel_spmd
from contextlib import ExitStack

F32 = mybir.dt.float32
BF16 = mybir.dt.bfloat16
AF = mybir.ActivationFunctionType
ALU = mybir.AluOpType
AX = mybir.AxisListType.X

P = 128
D = 2048
H = 1024
KC = D // P
EPS = 1e-5
CH1 = 8
CH2 = 4
FRING = 10
ARING = 4
RG = [[0, 1], [2, 3], [4, 5], [6, 7]]


class Ctr:
    def __init__(self, h):
        self.h = h
        self.n = 0

    def inc(self, ins, by=1):
        ins.then_inc(self.h, by)
        self.n += by
        return (self, self.n)


class Waits:
    def __init__(self):
        self.last = {}

    def __call__(self, eng, tok):
        if tok is None:
            return
        ctr, val = tok
        if val is None or val <= 0:
            return
        key = (str(eng.engine), id(ctr))
        if self.last.get(key, 0) >= val:
            return
        eng.wait_ge(ctr.h, val)
        self.last[key] = val


def build_nc(SEQ, dbg=False):
    NT = SEQ // P
    NCH1 = NT // CH1
    NCH2 = NT // CH2
    RPC1 = CH1 * P
    RPC2 = CH2 * P

    nc = bass.Bass("TRN2", target_bir_lowering=False, debug=False)

    xT = nc.dram_tensor("xT", [D, SEQ], F32, kind="ExternalInput").ap()
    x_in = nc.dram_tensor("x", [SEQ, D], F32, kind="ExternalInput").ap()
    xh_in = nc.dram_tensor("xh", [SEQ, H], F32, kind="ExternalInput").ap()
    w_in = {nm: nc.dram_tensor(nm, [D, H], F32, kind="ExternalInput").ap()
            for nm in ("wq", "wk", "wv", "wg", "wf")}
    b_in = {nm: nc.dram_tensor(nm, [1, H], F32, kind="ExternalInput").ap()
            for nm in ("bq", "bk", "bv", "bg", "bf")}
    tri_in = nc.dram_tensor("tri", [P, P], F32, kind="ExternalInput").ap()
    id32_in = nc.dram_tensor("id32", [P, P], F32, kind="ExternalInput").ap()
    idbf_in = nc.dram_tensor("idbf", [P, P], BF16, kind="ExternalInput").ap()
    onc_in = nc.dram_tensor("onc", [P, 1], BF16, kind="ExternalInput").ap()
    onr_in = nc.dram_tensor("onr", [1, P], F32, kind="ExternalInput").ap()
    out_d = nc.dram_tensor("out", [SEQ, H], F32, kind="ExternalOutput").ap()

    asp = [nc.dram_tensor(f"asp{c}", [RPC1, H], F32) for c in range(NCH1)]
    agt = [nc.dram_tensor(f"agt{c}", [2 * RPC1, H], F32) for c in range(NCH1)]
    msp = [nc.dram_tensor(f"msp{c}", [RPC1, 4], F32) for c in range(NCH1)]
    mgt = [nc.dram_tensor(f"mgt{c}", [2 * RPC1, 4], F32) for c in range(NCH1)]
    mlc = nc.dram_tensor("mlc", [SEQ, 2], F32)
    m4sp = [nc.dram_tensor(f"m4sp{c}", [2 * RPC2, 1], F32) for c in range(NCH2)]
    m4gt = [nc.dram_tensor(f"m4gt{c}", [4 * RPC2, 1], F32) for c in range(NCH2)]

    W = Waits()

    with ExitStack() as top:
        E = top.enter_context
        s_tri = E(nc.sbuf_tensor("s_tri", [P, P], F32))
        s_id32 = E(nc.sbuf_tensor("s_id32", [P, P], F32))
        s_idbf = E(nc.sbuf_tensor("s_idbf", [P, P], BF16))
        s_onc = E(nc.sbuf_tensor("s_onc", [P, 1], BF16))
        s_onr = E(nc.sbuf_tensor("s_onr", [1, P], F32))
        s_eps = E(nc.sbuf_tensor("s_eps", [P, 1], F32))
        s_bb = E(nc.sbuf_tensor("s_bb", [1, H], BF16))
        s_ex = {nm: E(nc.sbuf_tensor(f"s_ex_{nm}", [2, H], BF16))
                for nm in ("q", "k", "v", "g", "f")}

        sV = Ctr(E(nc.semaphore("sV")))
        sS = Ctr(E(nc.semaphore("sS")))
        sT = Ctr(E(nc.semaphore("sT")))
        sG = Ctr(E(nc.semaphore("sG")))
        vb = Ctr(E(nc.semaphore("vb")))
        cca = Ctr(E(nc.semaphore("cca")))
        ccm = Ctr(E(nc.semaphore("ccm")))
        cc4 = Ctr(E(nc.semaphore("cc4")))
        dcn = Ctr(E(nc.semaphore("dcn")))
        dw = [Ctr(E(nc.semaphore(f"dw{b}"))) for b in range(2)]
        dxT = Ctr(E(nc.semaphore("dxT")))
        dx = Ctr(E(nc.semaphore("dx")))
        dxh = [Ctr(E(nc.semaphore(f"dxh{b}"))) for b in range(2)]
        dcy = Ctr(E(nc.semaphore("dcy")))
        dsp = [Ctr(E(nc.semaphore(f"dsp{b}"))) for b in range(ARING)]
        dmo = [Ctr(E(nc.semaphore(f"dmo{b}"))) for b in range(4)]
        dml = [Ctr(E(nc.semaphore(f"dml{b}"))) for b in range(2)]
        dx2 = [Ctr(E(nc.semaphore(f"dx2{b}"))) for b in range(2)]
        daf = [Ctr(E(nc.semaphore(f"daf{b}"))) for b in range(2)]
        dam = [Ctr(E(nc.semaphore(f"dam{b}"))) for b in range(2)]
        dm2 = [Ctr(E(nc.semaphore(f"dm2{b}"))) for b in range(2)]
        dml2 = [Ctr(E(nc.semaphore(f"dml2{b}"))) for b in range(2)]
        dmo4 = [Ctr(E(nc.semaphore(f"dmo4{b}"))) for b in range(4)]
        dmo4b = [Ctr(E(nc.semaphore(f"dmo4b{b}"))) for b in range(4)]
        dm4 = [Ctr(E(nc.semaphore(f"dm4{b}"))) for b in range(4)]
        dso = [Ctr(E(nc.semaphore(f"dso{b}"))) for b in range(4)]
        dal = Ctr(E(nc.semaphore("dal")))

        block = E(nc.Block())

        def dma(eng, ctr, dst, src):
            ins = eng.dma_start(out=dst, in_=src)
            return ctr.inc(ins, 16)

        def vt(v, ins):
            _, n = vb.inc(ins)
            v.wait_ge(vb.h, n)
            return (vb, n)

        with ExitStack() as ph1:
            E1 = ph1.enter_context
            s_w = {nm: E1(nc.sbuf_tensor(f"s_{nm}", [P, KC * H], BF16))
                   for nm in ("wq", "wk", "wv", "wg")}
            s_xt = E1(nc.sbuf_tensor("s_xt", [P, D], F32))
            s_xTf = E1(nc.sbuf_tensor("s_xTf", [P, D], F32))
            s_xTb = [E1(nc.sbuf_tensor(f"s_xTb{b}", [P, D], BF16)) for b in range(2)]
            s_xhp = [E1(nc.sbuf_tensor(f"s_xhp{b}", [P, H], F32)) for b in range(2)]
            s_ks = E1(nc.sbuf_tensor("s_ks", [P, H], F32))
            s_gs = E1(nc.sbuf_tensor("s_gs", [P, H], F32))
            s_kv = E1(nc.sbuf_tensor("s_kv", [P, H], F32))
            s_qg = s_ks
            s_at = [E1(nc.sbuf_tensor(f"s_at{b}", [P, H], F32)) for b in range(ARING)]
            s_mo = [E1(nc.sbuf_tensor(f"s_mo{b}", [P, 4], F32)) for b in range(4)]
            s_cy = E1(nc.sbuf_tensor("s_cy", [1, H], F32))
            s_csb = s_kv
            s_st = [E1(nc.sbuf_tensor(f"s_st{b}", [P, 8], F32)) for b in range(2)]
            s_mc = [E1(nc.sbuf_tensor(f"s_mc{b}", [P, 2], F32)) for b in range(2)]
            s_mr = [E1(nc.sbuf_tensor(f"s_mr{b}", [2, P], BF16)) for b in range(2)]
            ps = E1(nc.psum_tensor("ps", [P, 4096], F32))
            pv = ps[:, 0:H]
            pq = ps[:, H:2 * H]
            pk = ps[:, 2 * H:3 * H]
            pg = ps[:, 3 * H:4 * H]
            pcs = pk

            @block.sync
            def _(s):
                for src, dst in ((tri_in, s_tri[:, :]), (id32_in, s_id32[:, :]),
                                 (idbf_in, s_idbf[:, :]), (onc_in, s_onc[:, :]),
                                 (onr_in, s_onr[:, :])):
                    dma(s, dcn, dst, src)

            cast_tok = {}
            rowdma = {}
            bias_hold = [None]
            for bi, nm in enumerate(("bq", "bk", "bv", "bg")):
                @block.sync
                def _(s, nm=nm, bi=bi):
                    if bi > 0:
                        W(s, rowdma[bi - 1])
                        W(s, cast_tok[bi - 1])
                    bias_hold[0] = dma(s, dcn, s_xTf[0:1, 0:H], b_in[nm])

                @block.vector
                def _(v, nm=nm, bi=bi):
                    if bi == 0:
                        v.memset(s_eps[:, :], EPS)
                        for b in range(4):
                            v.memset(s_mo[b][:, :], 0.0)
                    W(v, bias_hold[0])
                    cast_tok[bi] = sV.inc(
                        v.tensor_copy(s_bb[0:1, :], s_xTf[0:1, 0:H]))

                @block.sync
                def _(s, nm=nm, bi=bi):
                    W(s, cast_tok[bi])
                    rowdma[bi] = dma(s, dcn, s_ex[nm[1]][1:2, :], s_bb[0:1, :])
            consts_all = (dcn, dcn.n)

            wcast_tok = {}
            wld_hold = [None]
            for idx in range(4 * KC):
                nm = ("wq", "wk", "wv", "wg")[idx // KC]
                k = idx % KC
                slot = idx % 2
                stg = s_xt[:, slot * H:(slot + 1) * H]

                @block.sync
                def _(s, nm=nm, k=k, stg=stg, idx=idx, slot=slot):
                    if idx >= 2:
                        W(s, wcast_tok[idx - 2])
                    wld_hold[0] = dma(s, dw[slot], stg,
                                      w_in[nm][k * P:(k + 1) * P, :])

                @block.vector
                def _(v, nm=nm, k=k, stg=stg, idx=idx):
                    W(v, wld_hold[0])
                    wcast_tok[idx] = sV.inc(
                        v.tensor_copy(s_w[nm][:, k * H:(k + 1) * H], stg))

            sw_ps = {"q": ps[0:1, H:2 * H], "k": ps[0:1, 2 * H:3 * H],
                     "v": ps[0:1, 0:H], "g": ps[0:1, 3 * H:4 * H]}
            sw_mm_tok = {}
            for wi, nm in enumerate(("wq", "wk", "wv", "wg")):
                @block.tensor
                def _(t, nm=nm, wi=wi):
                    W(t, wcast_tok[wi * KC + KC - 1])
                    W(t, consts_all)
                    ins = None
                    for k in range(KC):
                        for n in range(2):
                            ins = t.matmul(
                                sw_ps[nm[1]][:, n * 512:(n + 1) * 512], s_onc[:, :],
                                s_w[nm][:, k * H + n * 512:k * H + (n + 1) * 512],
                                start=(k == 0), stop=(k == KC - 1))
                    sw_mm_tok[nm[1]] = sT.inc(ins)

            setup_done = [None]

            @block.vector
            def _(v):
                tok = None
                for nm in ("q", "k", "v", "g"):
                    W(v, sw_mm_tok[nm])
                    tok = sV.inc(v.tensor_copy(s_ex[nm][0:1, :], sw_ps[nm]))
                setup_done[0] = tok
            setup_v_tok = setup_done[0]

            tokV, tokS, tokT, tokD = {}, {}, {}, {}

            def proj(t, xtb, wsb, pdst):
                for k in range(KC):
                    for n in range(2):
                        t.matmul(
                            pdst[:, n * 512:(n + 1) * 512],
                            xtb[:, k * P:(k + 1) * P],
                            wsb[:, k * H + n * 512:k * H + (n + 1) * 512],
                            start=(k == 0), stop=False)

            def proj_close(t, pdst, exname, mr):
                ins = None
                for n in range(2):
                    ins = t.matmul(pdst[:, n * 512:(n + 1) * 512], mr[:, :],
                                   s_ex[exname][:, n * 512:(n + 1) * 512],
                                   start=False, stop=True)
                return ins

            def emit_front(j):
                stj = s_st[j % 2]
                mcj = s_mc[j % 2]
                xtb = s_xTb[j % 2]

                @block.vector
                def _(v, j=j, stj=stj, xtb=xtb):
                    if j == 0:
                        W(v, setup_v_tok)
                    W(v, tokD[(j, "xT")])
                    if j >= 2:
                        W(v, tokT[(j - 2, "g")])
                    tokV[(j, "cast")] = sV.inc(v.tensor_copy(xtb[:, :], s_xTf[:, :]))
                    W(v, tokD[(j, "x")])
                    if j >= 2:
                        W(v, tokD[(j - 2, "ml")])
                    vt(v, v.reduce_sum(stj[:, 0:1], s_xt[:, :], axis=AX))
                    vt(v, v.tensor_scalar(stj[:, 2:3], stj[:, 0:1], 1.0 / D, None,
                                          ALU.mult))
                    vt(v, v.tensor_tensor(stj[:, 3:4], stj[:, 2:3], stj[:, 2:3],
                                          ALU.mult))
                    vt(v, v.scalar_tensor_tensor(s_xTf[:, :], s_xt[:, :], 1.0,
                                                 s_xt[:, :], ALU.mult, ALU.mult,
                                                 accum_out=stj[:, 1:2]))
                    tokV[(j, "var")] = sV.inc(
                        v.tensor_scalar(stj[:, 4:5], stj[:, 1:2], 1.0 / D,
                                        stj[:, 3:4], ALU.mult, ALU.subtract))

                @block.scalar
                def _(s, j=j, stj=stj, mcj=mcj):
                    W(s, tokV[(j, "var")])
                    if j >= 2:
                        W(s, tokT[(j - 2, "tr")])
                    tokS[(j, "std")] = sS.inc(
                        s.activation(mcj[:, 1:2], stj[:, 4:5], AF.Sqrt,
                                     bias=s_eps[:, :]))

                @block.vector
                def _(v, j=j, stj=stj, mcj=mcj):
                    W(v, tokS[(j, "std")])
                    vt(v, v.reciprocal(stj[:, 5:6], mcj[:, 1:2]))
                    vt(v, v.tensor_tensor(stj[:, 6:7], stj[:, 5:6], stj[:, 5:6],
                                          ALU.mult))
                    tokV[(j, "stat")] = sV.inc(
                        v.tensor_scalar(mcj[:, 0:1], stj[:, 2:3], -1.0, None,
                                        ALU.mult))

            @block.sync
            def _(s):
                W(s, rowdma[3])
                W(s, cast_tok[3])
                src = xT.rearrange("(k p) q -> p k q", p=P)[:, :, 0:P]
                dst = s_xTf[:, :].rearrange("p (k c) -> p k c", k=KC)
                tokD[(0, "xT")] = dma(s, dxT, dst, src)
                W(s, wcast_tok[4 * KC - 1])
                tokD[(0, "x")] = dma(s, dx, s_xt[:, :], x_in[0:P, :])
                tokD[(0, "xh")] = dma(s, dxh[0], s_xhp[0][:, :], xh_in[0:P, :])
            emit_front(0)

            for i in range(NT):
                xtb = s_xTb[i % 2]
                atb = s_at[i % ARING]
                mob = s_mo[i % 4]
                xhb = s_xhp[i % 2]
                sti = s_st[i % 2]
                mri = s_mr[i % 2]
                rstd = sti[:, 5:6]
                rstd2 = sti[:, 6:7]

                if i + 1 < NT:
                    @block.sync
                    def _(s, i=i):
                        j = i + 1
                        W(s, tokV[(i, "var")])
                        src = xT.rearrange("(k p) q -> p k q",
                                           p=P)[:, :, j * P:(j + 1) * P]
                        dst = s_xTf[:, :].rearrange("p (k c) -> p k c", k=KC)
                        tokD[(j, "xT")] = dma(s, dxT, dst, src)
                        W(s, tokV[(i, "var")])
                        tokD[(j, "x")] = dma(s, dx, s_xt[:, :],
                                             x_in[j * P:(j + 1) * P, :])
                        if j >= 2:
                            W(s, tokV[(j - 2, "cross")])
                        tokD[(j, "xh")] = dma(s, dxh[j % 2], s_xhp[j % 2][:, :],
                                              xh_in[j * P:(j + 1) * P, :])

                @block.tensor
                def _(t, i=i, xtb=xtb, mri=mri):
                    if i == 0:
                        W(t, setup_v_tok)
                        W(t, consts_all)
                    W(t, tokV[(i, "cast")])
                    if i > 0:
                        W(t, tokV[(i - 1, "kv")])
                    proj(t, xtb, s_w["wv"], pv)
                    W(t, tokV[(i, "stat")])
                    if i > 0:
                        W(t, tokS[(i - 1, "gs")])
                    ins = t.transpose(ps[0:2, 3 * H:3 * H + P], s_mc[i % 2][:, :],
                                      s_id32[:, :])
                    tokT[(i, "tr")] = sT.inc(ins)

                @block.vector
                def _(v, i=i, mri=mri):
                    W(v, tokT[(i, "tr")])
                    if i >= 2:
                        W(v, tokT[(i - 2, "g")])
                    tokV[(i, "mr")] = sV.inc(
                        v.tensor_copy(mri[:, :], ps[0:2, 3 * H:3 * H + P]))

                if i + 1 < NT:
                    emit_front(i + 1)

                @block.tensor
                def _(t, i=i, xtb=xtb, mri=mri):
                    if i > 0:
                        W(t, tokV[(i - 1, "qg")])
                    proj(t, xtb, s_w["wq"], pq)
                    W(t, tokV[(i, "mr")])
                    tokT[(i, "v")] = sT.inc(proj_close(t, pv, "v", mri))
                    tokT[(i, "q")] = sT.inc(proj_close(t, pq, "q", mri))
                    if i > 0:
                        W(t, tokV[(i - 1, "csb")])
                    proj(t, xtb, s_w["wk"], pk)
                    tokT[(i, "k")] = sT.inc(proj_close(t, pk, "k", mri))
                    proj(t, xtb, s_w["wg"], pg)
                    tokT[(i, "g")] = sT.inc(proj_close(t, pg, "g", mri))

                @block.scalar
                def _(s, i=i, rstd=rstd):
                    W(s, tokT[(i, "g")])
                    if i > 0:
                        W(s, tokV[(i - 1, "qg")])
                    tokS[(i, "gs")] = sS.inc(
                        s.activation(s_gs[:, :], pg[:, :], AF.Sigmoid, scale=rstd))

                @block.vector
                def _(v, i=i, rstd2=rstd2):
                    W(v, tokT[(i, "k")])
                    tokV[(i, "ks")] = vt(v, v.tensor_scalar(
                        s_ks[:, :], pk[:, :], rstd2, None, ALU.mult))
                    W(v, tokT[(i, "v")])
                    if i > 0:
                        W(v, tokT[(i - 1, "cs")])
                        W(v, tokD[(i - 1, "cy")])
                    tokV[(i, "kv")] = vt(v, v.tensor_tensor(
                        s_kv[:, :], s_ks[:, :], pv[:, :], ALU.mult))

                @block.tensor
                def _(t, i=i):
                    W(t, tokV[(i, "kv")])
                    W(t, tokV[(i, "ks")])
                    if i > 0:
                        W(t, tokD[(i - 1, "cy")])
                    ins = None
                    for n in range(2):
                        ins = t.matmul(pcs[:, n * 512:(n + 1) * 512], s_tri[:, :],
                                       s_kv[:, n * 512:(n + 1) * 512],
                                       start=True, stop=(i == 0))
                        if i > 0:
                            ins = t.matmul(pcs[:, n * 512:(n + 1) * 512],
                                           s_onr[:, :],
                                           s_cy[:, n * 512:(n + 1) * 512],
                                           start=False, stop=True)
                    tokT[(i, "cs")] = sT.inc(ins)

                @block.vector
                def _(v, i=i, atb=atb, mob=mob, xhb=xhb, rstd=rstd):
                    W(v, tokT[(i, "q")])
                    W(v, tokS[(i, "gs")])
                    tokV[(i, "qg")] = vt(v, v.scalar_tensor_tensor(
                        s_qg[:, :], pq[:, :], rstd, s_gs[:, :],
                        ALU.mult, ALU.mult))
                    W(v, tokT[(i, "cs")])
                    if i >= ARING:
                        W(v, tokD[(i - ARING, "at")])
                    tokV[(i, "attn")] = vt(v, v.tensor_tensor(
                        atb[:, :], s_qg[:, :], pcs[:, :], ALU.mult))
                    if i >= 4:
                        W(v, tokD[(i - 4, "mo")])
                    sV.inc(v.reduce_sum(mob[:, 0:1], atb[:, :], axis=AX))
                    W(v, tokD[(i, "xh")])
                    tokV[(i, "cross")] = vt(v, v.scalar_tensor_tensor(
                        s_kv[:, :], xhb[:, :], 1.0, atb[:, :], ALU.mult, ALU.mult,
                        accum_out=mob[:, 2:3]))
                    tokV[(i, "asq")] = vt(v, v.scalar_tensor_tensor(
                        s_kv[:, :], atb[:, :], 1.0, atb[:, :], ALU.mult, ALU.mult,
                        accum_out=mob[:, 1:2]))
                    if i > 0:
                        W(v, tokD[(i - 1, "cy")])
                    tokV[(i, "csb")] = vt(v, v.tensor_copy(
                        s_csb[:, :], pcs[:, :]))

                @block.gpsimd
                def _(g, i=i, atb=atb, mob=mob, sti=sti):
                    c = i // CH1
                    r = (i % CH1) * P
                    W(g, tokV[(i, "csb")])
                    tokD[(i, "cy")] = dma(g, dcy, s_cy[:, :], s_csb[P - 1:P, :])
                    W(g, tokV[(i, "attn")])
                    tokD[(i, "at")] = dma(g, dsp[i % ARING],
                                          asp[c].ap()[r:r + P, :], atb[:, :])
                    W(g, tokV[(i, "asq")])
                    tokD[(i, "mo")] = dma(g, dmo[i % 4],
                                          msp[c].ap()[r:r + P, :], mob[:, :])
                    tokD[(i, "ml")] = dma(g, dml[i % 2],
                                          mlc.ap()[i * P:(i + 1) * P, :],
                                          sti[:, 0:2])
                    if i % CH1 == CH1 - 1:
                        for b in range(ARING):
                            W(g, tokD[(i - ((i - b) % ARING), "at")])
                        for b in range(4):
                            W(g, tokD[(i - ((i - b) % 4), "mo")])
                        cca.inc(g.collective_compute(
                            "AllGather", ALU.bypass, replica_groups=RG,
                            ins=[asp[c].ap().opt()], outs=[agt[c].ap().opt()]))
                        ccm.inc(g.collective_compute(
                            "AllGather", ALU.bypass, replica_groups=RG,
                            ins=[msp[c].ap().opt()], outs=[mgt[c].ap().opt()]))

            ph1_end = dict(v=(sV, sV.n), s=(sS, sS.n), t=(sT, sT.n))
            ph1_mltok = {i: tokD[(i, "ml")] for i in range(NT)}
            ph1_dmas = [(c, c.n) for c in
                        ([dcy, dxT, dx, dcn] + dw + dxh + dsp + dmo + dml)]

        with ExitStack() as ph2:
            E2 = ph2.enter_context
            s_wf = E2(nc.sbuf_tensor("s_wf", [P, KC * H], BF16))
            s_x2 = [E2(nc.sbuf_tensor(f"s_x2_{b}", [P, D], F32)) for b in range(1)]
            s_af = [E2(nc.sbuf_tensor(f"s_af{b}", [P, D], F32)) for b in range(1)]
            s_am = [E2(nc.sbuf_tensor(f"s_am{b}", [P, H], F32)) for b in range(2)]
            s_xh = [E2(nc.sbuf_tensor(f"s_xh{b}", [P, H], F32)) for b in range(1)]
            s_l2 = E2(nc.sbuf_tensor("s_l2", [P, D], F32))
            s_l2m = E2(nc.sbuf_tensor("s_l2m", [P, H], F32))
            s_yb = [E2(nc.sbuf_tensor(f"s_yb{b}", [P, D], BF16)) for b in range(2)]
            s_yT = [E2(nc.sbuf_tensor(f"s_yT{b}", [P, D], BF16)) for b in range(2)]
            s_fr = [E2(nc.sbuf_tensor(f"s_fr{b}", [P, H], F32))
                    for b in range(FRING)]
            s_s1 = [E2(nc.sbuf_tensor(f"s_s1_{b}", [P, H], F32))
                    for b in range(FRING)]
            s_ot = [E2(nc.sbuf_tensor(f"s_ot{b}", [P, H], F32)) for b in range(4)]
            s_m2 = [E2(nc.sbuf_tensor(f"s_m2_{b}", [P, 8], F32)) for b in range(2)]
            s_mlb = [E2(nc.sbuf_tensor(f"s_mlb{b}", [P, 2], F32)) for b in range(2)]
            s_m4 = [E2(nc.sbuf_tensor(f"s_m4_{b}", [P, 12], F32)) for b in range(4)]
            s_mo4a = [E2(nc.sbuf_tensor(f"s_mo4a{b}", [P, 1], F32)) for b in range(4)]
            s_mo4b = [E2(nc.sbuf_tensor(f"s_mo4b{b}", [P, 1], F32)) for b in range(4)]
            s_st2 = [E2(nc.sbuf_tensor(f"s_st2_{b}", [P, 24], F32))
                     for b in range(2)]
            s_mc2 = [E2(nc.sbuf_tensor(f"s_mc2_{b}", [P, 2], F32))
                     for b in range(2)]
            s_mcb = [E2(nc.sbuf_tensor(f"s_mcb{b}", [P, 2], BF16))
                     for b in range(2)]
            s_mr2 = [E2(nc.sbuf_tensor(f"s_mr2{i}", [2, P], BF16)) for i in range(2)]
            s_dmp = E2(nc.sbuf_tensor("s_dmp", [P, H], F32))
            s_dsc = E2(nc.sbuf_tensor("s_dsc", [P, 128], F32))
            s_stg = [s_x2[0][:, 0:H], s_x2[0][:, H:D]]
            s_abs = [s_yT[0][:, 0:H], s_yT[0][:, H:D]]
            s_alr = s_l2[0:1, 0:H]
            s_bfs = s_alr
            s_alb = s_yb[0][0:1, 0:H]
            s_alf = s_yb[0][:, H:D]
            s_ssr = s_l2m[0:1, 0:H]
            ps2 = E2(nc.psum_tensor("ps2", [P, 2048], F32))
            psb = E2(nc.psum_tensor("psb", [P, 2176], BF16))

            def bwait(eng):
                W(eng, ph1_end["t"])
                W(eng, ph1_end["v"])
                W(eng, ph1_end["s"])
                for tok in ph1_dmas:
                    W(eng, tok)

            wfl_hold = [None]
            abs_tok, sgn_tok = {}, {}
            for k in range(KC):
                slot = k % 2

                @block.sync
                def _(s, k=k, slot=slot):
                    bwait(s)
                    if k >= 2:
                        W(s, sgn_tok[k - 2])
                    wfl_hold[0] = dma(s, dw[slot], s_stg[slot],
                                      w_in["wf"][k * P:(k + 1) * P, :])

                @block.scalar
                def _(s, k=k, slot=slot):
                    bwait(s)
                    W(s, wfl_hold[0])
                    if k >= 2:
                        W(s, abs_tok[k - 2])
                    sS.inc(s.activation(s_wf[:, k * H:(k + 1) * H], s_stg[slot],
                                        AF.Sign))
                    sgn_tok[k] = sS.inc(
                        s.activation(s_abs[slot][:, :], s_stg[slot], AF.Abs))

                @block.tensor
                def _(t, k=k, slot=slot):
                    bwait(t)
                    W(t, sgn_tok[k])
                    ins = None
                    for n in range(2):
                        ins = t.matmul(ps2[0:1, n * 512:(n + 1) * 512], s_onc[:, :],
                                       s_abs[slot][:, n * 512:(n + 1) * 512],
                                       start=(k == 0), stop=(k == KC - 1))
                    abs_tok[k] = sT.inc(ins)

            ss_tok = [None]

            @block.tensor
            def _(t):
                W(t, sgn_tok[KC - 1])
                ins = None
                for k in range(KC):
                    for n in range(2):
                        ins = t.matmul(ps2[0:1, H + n * 512:H + (n + 1) * 512],
                                       s_onc[:, :],
                                       s_wf[:, k * H + n * 512:k * H + (n + 1) * 512],
                                       start=(k == 0), stop=(k == KC - 1))
                ss_tok[0] = sT.inc(ins)

            alb_tok = [None]

            @block.vector
            def _(v):
                bwait(v)
                W(v, abs_tok[KC - 1])
                vt(v, v.tensor_scalar(s_alr, ps2[0:1, 0:H], 1.0 / D, None, ALU.mult))
                alb_tok[0] = sV.inc(v.tensor_copy(s_alb, s_alr))

            alf_tok = [None]

            @block.gpsimd
            def _(g):
                bwait(g)
                W(g, alb_tok[0])
                tok = dma(g, dal, s_alf[0:1, :], s_alb)
                rows = 1
                while rows < P:
                    W(g, tok)
                    tok = dma(g, dal, s_alf[rows:2 * rows, :], s_alf[0:rows, :])
                    rows *= 2
                alf_tok[0] = tok

            wfa_tok = {}
            for k in range(KC):
                @block.vector
                def _(v, k=k):
                    W(v, alf_tok[0])
                    W(v, ss_tok[0])
                    wfa_tok[k] = sV.inc(
                        v.tensor_tensor(s_wf[:, k * H:(k + 1) * H],
                                        s_wf[:, k * H:(k + 1) * H],
                                        s_alf[:, :], ALU.mult))

            sbwrow_tok = [None]

            @block.vector
            def _(v):
                W(v, ss_tok[0])
                vt(v, v.tensor_copy(s_ssr, ps2[0:1, H:2 * H]))
                vt(v, v.tensor_tensor(s_ssr, s_ssr, s_alr, ALU.mult))
                sbwrow_tok[0] = sV.inc(v.tensor_copy(s_ex["f"][0:1, :], s_ssr))

            bfl = [None]

            @block.sync
            def _(s):
                W(s, sbwrow_tok[0])
                bfl[0] = dma(s, dcn, s_bfs, b_in["bf"])

            exf_cast = [None]

            @block.vector
            def _(v):
                W(v, bfl[0])
                exf_cast[0] = sV.inc(v.tensor_copy(s_bb[0:1, :], s_bfs))

            exf_row = [None]

            @block.sync
            def _(s):
                W(s, exf_cast[0])
                exf_row[0] = dma(s, dcn, s_ex["f"][1:2, :], s_bb[0:1, :])

            tV, tS, tT, tD, tG = {}, {}, {}, {}, {}

            def emit_stats(j):
                c1 = j // CH1
                r1 = (j % CH1) * P
                c2 = j // CH2
                x2b = s_x2[0]
                afb = s_af[0]
                amb = s_am[j % 2]
                xhb = s_xh[0]
                m2b = s_m2[j % 2]
                mlb = s_mlb[j % 2]
                frb = s_fr[j % FRING]
                s1b = s_s1[j % FRING]
                moa = s_mo4a[j % 4]
                mosb = s_mo4b[j % 4]
                pf = ps2[:, (j % 2) * H:(j % 2 + 1) * H]
                st = s_st2[j % 2]
                mc2 = s_mc2[j % 2]
                mcb = s_mcb[j % 2]

                @block.sync
                def _(s, j=j, c1=c1, r1=r1, x2b=x2b, afb=afb, amb=amb, xhb=xhb,
                      m2b=m2b, mlb=mlb):
                    if j < 2:
                        W(s, sgn_tok[KC - 1])
                        W(s, exf_row[0])
                    if j >= 1:
                        W(s, tG[(j - 1, "y")])
                    tD[(j, "x")] = dma(s, dx2[0], x2b[:, :],
                                       x_in[j * P:(j + 1) * P, :])
                    W(s, (cca, c1 + 1))
                    if j >= 1:
                        W(s, tV[(j - 1, "l2")])
                    dma(s, daf[0], afb[:, 0:H], agt[c1].ap()[r1:r1 + P, :])
                    tD[(j, "af")] = dma(s, daf[0], afb[:, H:D],
                                        agt[c1].ap()[RPC1 + r1:RPC1 + r1 + P, :])


                @block.scalar
                def _(s, j=j, c1=c1, r1=r1, m2b=m2b, mlb=mlb):
                    if j == 0:
                        bwait(s)
                    W(s, (ccm, c1 + 1))
                    if j >= 2:
                        W(s, tV[(j - 2, "m2c")])
                    dma(s, dm2[j % 2], m2b[:, 0:4], mgt[c1].ap()[r1:r1 + P, :])
                    tD[(j, "m2")] = dma(s, dm2[j % 2], m2b[:, 4:8],
                                        mgt[c1].ap()[RPC1 + r1:RPC1 + r1 + P, :])
                    if j >= 2:
                        W(s, tV[(j - 2, "alg")])
                    W(s, ph1_mltok[j])
                    tD[(j, "ml")] = dma(s, dml2[j % 2], mlb[:, :],
                                        mlc.ap()[j * P:(j + 1) * P, :])

                @block.vector
                def _(v, j=j, m2b=m2b, st=st):
                    if j == 0:
                        bwait(v)
                    W(v, tD[(j, "m2")])
                    vt(v, v.tensor_tensor(st[:, 0:4], m2b[:, 0:4], m2b[:, 4:8],
                                          ALU.add))
                    vt(v, v.tensor_scalar(st[:, 4:5], st[:, 0:1], 1.0 / D, None,
                                          ALU.mult))
                    vt(v, v.tensor_tensor(st[:, 5:6], st[:, 4:5], st[:, 4:5],
                                          ALU.mult))
                    tV[(j, "m2c")] = sV.inc(
                        v.tensor_scalar(st[:, 6:7], st[:, 1:2], 1.0 / D,
                                        st[:, 5:6], ALU.mult, ALU.subtract))

                @block.scalar
                def _(s, j=j, st=st, c1=c1, r1=r1, amb=amb):
                    if j == 0:
                        bwait(s)
                    s.activation(s_dsc[:, (2 * j) % 128:(2 * j) % 128 + 1],
                                 s_eps[:, :], AF.Sqrt, bias=s_eps[:, :])
                    W(s, tV[(j, "m2c")])
                    tS[(j, "std2")] = sS.inc(
                        s.activation(st[:, 7:8], st[:, 6:7], AF.Sqrt,
                                     bias=s_eps[:, :]))
                    if j >= 2:
                        W(s, tV[(j - 2, "l2m")])
                    tD[(j, "am")] = dma(s, dam[j % 2], amb[:, :],
                                        asp[c1].ap()[r1:r1 + P, :])

                @block.vector
                def _(v, j=j, st=st, afb=afb, amb=amb):
                    W(v, tS[(j, "std2")])
                    vt(v, v.reciprocal(st[:, 8:9], st[:, 7:8]))
                    tV[(j, "r2")] = vt(v, v.tensor_scalar(
                        st[:, 9:10], st[:, 4:5], st[:, 8:9], -1.0,
                        ALU.mult, ALU.mult))
                    W(v, tD[(j, "af")])
                    if j >= 1:
                        W(v, tG[(j - 1, "y")])
                    tV[(j, "l2a")] = sV.inc(
                        v.tensor_scalar(s_l2[:, 0:H], afb[:, 0:H], st[:, 8:9],
                                        st[:, 9:10], ALU.mult, ALU.add))
                    tV[(j, "l2")] = sV.inc(
                        v.tensor_scalar(s_l2[:, H:D], afb[:, H:D], st[:, 8:9],
                                        st[:, 9:10], ALU.mult, ALU.add))
                    W(v, tD[(j, "am")])
                    if j >= 1:
                        W(v, tG[(j - 1, "s1")])
                    tV[(j, "l2m")] = sV.inc(
                        v.tensor_scalar(s_l2m[:, :], amb[:, :], st[:, 8:9],
                                        st[:, 9:10], ALU.mult, ALU.add))

                @block.vector
                def _(v, j=j, st=st, mlb=mlb):
                    W(v, tD[(j, "ml")])
                    W(v, tV[(j, "r2")])
                    v.tensor_scalar(st[:, 10:11], mlb[:, 0:1], 1.0 / D, None,
                                    ALU.mult)
                    v.tensor_scalar(st[:, 11:12], mlb[:, 0:1], st[:, 4:5],
                                    st[:, 2:3], ALU.mult, ALU.subtract)
                    v.tensor_scalar(st[:, 13:14], st[:, 5:6], float(D), None,
                                    ALU.mult)
                    vt(v, v.tensor_tensor(st[:, 15:16], st[:, 8:9], st[:, 8:9],
                                          ALU.mult))
                    v.tensor_scalar(st[:, 12:13], st[:, 11:12], st[:, 8:9],
                                    -2.0, ALU.mult, ALU.mult)
                    v.tensor_scalar(st[:, 14:15], st[:, 1:2], 1.0,
                                    st[:, 13:14], ALU.mult, ALU.subtract)
                    vt(v, v.tensor_tensor(st[:, 18:19], st[:, 10:11], st[:, 10:11],
                                          ALU.mult))
                    v.tensor_tensor(st[:, 16:17], st[:, 14:15], st[:, 15:16],
                                    ALU.mult)
                    vt(v, v.tensor_tensor(st[:, 17:18], mlb[:, 1:2], st[:, 12:13],
                                          ALU.add))
                    vt(v, v.tensor_tensor(st[:, 17:18], st[:, 17:18], st[:, 16:17],
                                          ALU.add))
                    tV[(j, "alg")] = sV.inc(
                        v.tensor_scalar(st[:, 19:20], st[:, 17:18], 1.0 / D,
                                        st[:, 18:19], ALU.mult, ALU.subtract))

                @block.scalar
                def _(s, j=j, st=st, xhb=xhb):
                    s.activation(s_dsc[:, (2 * j + 1) % 128:(2 * j + 1) % 128 + 1],
                                 s_eps[:, :], AF.Sqrt, bias=s_eps[:, :])
                    W(s, tV[(j, "alg")])
                    tS[(j, "std3")] = sS.inc(
                        s.activation(st[:, 20:21], st[:, 19:20], AF.Sqrt,
                                     bias=s_eps[:, :]))
                    if j >= 1:
                        W(s, tG[(j - 1, "s1")])
                    tD[(j, "xh")] = dma(s, dxh[0], xhb[:, :],
                                        xh_in[j * P:(j + 1) * P, :])

                @block.vector
                def _(v, j=j, st=st):
                    W(v, tS[(j, "std3")])
                    vt(v, v.reciprocal(st[:, 21:22], st[:, 20:21]))
                    vt(v, v.tensor_scalar(st[:, 22:23], st[:, 10:11], -1.0, None,
                                          ALU.mult))
                    vt(v, v.tensor_copy(mc2[:, 0:1], st[:, 22:23]))
                    vt(v, v.tensor_copy(mc2[:, 1:2], st[:, 20:21]))
                    if j >= 2:
                        W(v, tT[(j - 2, "trm")])
                    tV[(j, "mcb")] = sV.inc(v.tensor_copy(mcb[:, :], mc2[:, :]))

                @block.gpsimd
                def _(g, j=j, x2b=x2b, s1b=s1b, xhb=xhb):
                    if j == 0:
                        bwait(g)
                        W(g, sbwrow_tok[0])
                    W(g, tV[(j, "l2a")])
                    W(g, tD[(j, "x")])
                    if j >= 2:
                        W(g, tT[(j - 2, "tr")])
                    if j <= 1:
                        W(g, alf_tok[0])
                    tG[(j, "ya")] = sG.inc(
                        g.tensor_tensor(s_yb[j % 2][:, 0:H], x2b[:, 0:H],
                                        s_l2[:, 0:H], ALU.add))
                    W(g, tV[(j, "l2")])
                    tG[(j, "y")] = sG.inc(
                        g.tensor_tensor(s_yb[j % 2][:, H:D], x2b[:, H:D],
                                        s_l2[:, H:D], ALU.add))
                    W(g, tV[(j, "l2m")])
                    W(g, tD[(j, "xh")])
                    if j >= FRING:
                        W(g, tV[(j - FRING, "ot")])
                    tG[(j, "s1")] = sG.inc(
                        g.tensor_tensor(s1b[:, :], xhb[:, :], s_l2m[:, :], ALU.add))


            def emit_workA(j):
                c1 = j // CH1
                r1 = (j % CH1) * P
                c2 = j // CH2
                x2b = s_x2[0]
                afb = s_af[0]
                amb = s_am[j % 2]
                xhb = s_xh[0]
                m2b = s_m2[j % 2]
                mlb = s_mlb[j % 2]
                frb = s_fr[j % FRING]
                s1b = s_s1[j % FRING]
                moa = s_mo4a[j % 4]
                mosb = s_mo4b[j % 4]
                pf = ps2[:, (j % 2) * H:(j % 2 + 1) * H]
                st = s_st2[j % 2]
                mc2 = s_mc2[j % 2]
                mcb = s_mcb[j % 2]

                @block.tensor
                def _(t, j=j):
                    W(t, tG[(j, "ya")])
                    if j == 0:
                        W(t, ss_tok[0])
                    if j >= 1:
                        W(t, tV[(j - 1, "yTea")])
                    ins = None
                    for k in range(KC // 2):
                        ins = t.transpose(psb[:, k * P:(k + 1) * P],
                                          s_yb[j % 2][:, k * P:(k + 1) * P],
                                          s_idbf[:, :])
                    tT[(j, "tra")] = sT.inc(ins)
                    W(t, tG[(j, "y")])
                    if j >= 1:
                        W(t, tV[(j - 1, "yTe")])
                    for k in range(KC // 2, KC):
                        ins = t.transpose(psb[:, k * P:(k + 1) * P],
                                          s_yb[j % 2][:, k * P:(k + 1) * P],
                                          s_idbf[:, :])
                    tT[(j, "tr")] = sT.inc(ins)
                    W(t, tV[(j, "mcb")])
                    ins = t.transpose(psb[0:2, 2048:2176], s_mcb[j % 2][:, :],
                                      s_idbf[:, :])
                    tT[(j, "trm")] = sT.inc(ins)

                @block.vector
                def _(v, j=j):
                    W(v, tT[(j, "tra")])
                    if j <= 1:
                        for k in range(KC):
                            W(v, wfa_tok[k])
                        W(v, ss_tok[0])
                    if j >= 2:
                        W(v, tT[(j - 2, "f")])
                    tV[(j, "yTea")] = sV.inc(
                        v.tensor_copy(s_yT[j % 2][:, 0:H], psb[:, 0:H]))
                    W(v, tT[(j, "tr")])
                    tV[(j, "yTe")] = sV.inc(
                        v.tensor_copy(s_yT[j % 2][:, H:D], psb[:, H:D]))
                    W(v, tT[(j, "trm")])
                    tV[(j, "mr2")] = sV.inc(
                        v.tensor_copy(s_mr2[j % 2][:, :], psb[0:2, 2048:2176]))

            def emit_workB(j):
                c1 = j // CH1
                r1 = (j % CH1) * P
                c2 = j // CH2
                x2b = s_x2[0]
                afb = s_af[0]
                amb = s_am[j % 2]
                xhb = s_xh[0]
                m2b = s_m2[j % 2]
                mlb = s_mlb[j % 2]
                frb = s_fr[j % FRING]
                s1b = s_s1[j % FRING]
                moa = s_mo4a[j % 4]
                mosb = s_mo4b[j % 4]
                pf = ps2[:, (j % 2) * H:(j % 2 + 1) * H]
                st = s_st2[j % 2]
                mc2 = s_mc2[j % 2]
                mcb = s_mcb[j % 2]

                @block.tensor
                def _(t, j=j, pf=pf):
                    W(t, tV[(j, "yTe")])
                    for k in range(KC):
                        W(t, wfa_tok[k])
                    W(t, exf_row[0])
                    if j >= 2:
                        W(t, tV[(j - 2, "fev")])
                    for k in range(KC):
                        for n in range(2):
                            t.matmul(
                                pf[:, n * 512:(n + 1) * 512],
                                s_yT[j % 2][:, k * P:(k + 1) * P],
                                s_wf[:, k * H + n * 512:k * H + (n + 1) * 512],
                                start=(k == 0), stop=False)
                    W(t, tV[(j, "mr2")])
                    ins = None
                    for n in range(2):
                        ins = t.matmul(pf[:, n * 512:(n + 1) * 512], s_mr2[j % 2][:, :],
                                       s_ex["f"][:, n * 512:(n + 1) * 512],
                                       start=False, stop=True)
                    tT[(j, "f")] = sT.inc(ins)

                @block.vector
                def _(v, j=j, frb=frb, pf=pf, st=st, mosb=mosb, moa=moa):
                    W(v, tT[(j, "f")])
                    if j >= FRING:
                        W(v, tV[(j - FRING, "ot")])
                    tV[(j, "fev")] = vt(v, v.tensor_scalar(
                        frb[:, :], pf[:, :], st[:, 21:22], None, ALU.mult))
                    if j >= 4:
                        W(v, tD[(j - 4, "mo4a")])
                    tV[(j, "fred")] = sV.inc(
                        v.reduce_sum(moa[:, 0:1], frb[:, :], axis=AX))
                    if j >= 4:
                        W(v, tD[(j - 4, "mo4b")])
                    tV[(j, "fsq")] = sV.inc(
                        v.scalar_tensor_tensor(s_dmp[:, :], frb[:, :], 1.0,
                                               frb[:, :], ALU.mult, ALU.mult,
                                               accum_out=mosb[:, 0:1]))

                @block.gpsimd
                def _(g, j=j, frb=frb, moa=moa, xhb=xhb, s1b=s1b, c2=c2,
                      mosb=mosb):
                    r2 = (j % CH2) * P
                    W(g, tV[(j, "fred")])
                    tD[(j, "mo4a")] = dma(g, dmo4[j % 4],
                                          m4sp[c2].ap()[r2:r2 + P, :], moa[:, :])
                    W(g, tV[(j, "fsq")])
                    tD[(j, "mo4b")] = dma(g, dmo4b[j % 4],
                                          m4sp[c2].ap()[RPC2 + r2:RPC2 + r2 + P, :],
                                          mosb[:, :])
                    if j % CH2 == CH2 - 1:
                        for b in range(4):
                            W(g, tD[(j - ((j - b) % 4), "mo4b")])
                            W(g, tD[(j - ((j - b) % 4), "mo4a")])
                        cc4.inc(g.collective_compute(
                            "AllGather", ALU.bypass, replica_groups=RG,
                            ins=[m4sp[c2].ap().opt()], outs=[m4gt[c2].ap().opt()]))

            def emit_finalize_chunk(c2):
                jfs = list(range(c2 * CH2, (c2 + 1) * CH2))

                @block.sync
                def _(q, c2=c2, jfs=jfs):
                    W(q, (cc4, c2 + 1))
                    for jf in jfs:
                        r2 = (jf % CH2) * P
                        m4b = s_m4[jf % 4]
                        if jf >= 4:
                            W(q, tV[(jf - 4, "r4")])
                        dma(q, dm4[jf % 4], m4b[:, 0:1],
                            m4gt[c2].ap()[r2:r2 + P, :])
                        dma(q, dm4[jf % 4], m4b[:, 1:2],
                            m4gt[c2].ap()[RPC2 + r2:RPC2 + r2 + P, :])
                        dma(q, dm4[jf % 4], m4b[:, 2:3],
                            m4gt[c2].ap()[2 * RPC2 + r2:2 * RPC2 + r2 + P, :])
                        tD[(jf, "m4")] = dma(
                            q, dm4[jf % 4], m4b[:, 3:4],
                            m4gt[c2].ap()[3 * RPC2 + r2:3 * RPC2 + r2 + P, :])

                @block.vector
                def _(v, jfs=jfs):
                    for jf in jfs:
                        m4b = s_m4[jf % 4]
                        W(v, tD[(jf, "m4")])
                        vt(v, v.tensor_tensor(m4b[:, 4:6], m4b[:, 0:2],
                                              m4b[:, 2:4], ALU.add))
                        vt(v, v.tensor_scalar(m4b[:, 6:7], m4b[:, 4:5], 1.0 / D,
                                              None, ALU.mult))
                        vt(v, v.tensor_tensor(m4b[:, 7:8], m4b[:, 6:7],
                                              m4b[:, 6:7], ALU.mult))
                        tV[(jf, "var4")] = sV.inc(
                            v.tensor_scalar(m4b[:, 8:9], m4b[:, 5:6], 1.0 / D,
                                            m4b[:, 7:8], ALU.mult, ALU.subtract))

                @block.scalar
                def _(s, jfs=jfs):
                    for jf in jfs:
                        m4b = s_m4[jf % 4]
                        W(s, tV[(jf, "var4")])
                        tS[(jf, "std4")] = sS.inc(
                            s.activation(m4b[:, 11:12], m4b[:, 8:9], AF.Sqrt,
                                         bias=s_eps[:, :]))

                @block.vector
                def _(v, jfs=jfs):
                    for jf in jfs:
                        m4b = s_m4[jf % 4]
                        frb = s_fr[jf % FRING]
                        s1b = s_s1[jf % FRING]
                        otb = s_ot[jf % 4]
                        W(v, tS[(jf, "std4")])
                        vt(v, v.reciprocal(m4b[:, 9:10], m4b[:, 11:12]))
                        tV[(jf, "r4")] = vt(v, v.tensor_scalar(
                            m4b[:, 10:11], m4b[:, 6:7], m4b[:, 9:10], -1.0,
                            ALU.mult, ALU.mult))
                        vt(v, v.scalar_tensor_tensor(
                            s1b[:, :], frb[:, :], m4b[:, 9:10], s1b[:, :],
                            ALU.mult, ALU.add))
                        if jf >= 4:
                            W(v, tD[(jf - 4, "od")])
                        tV[(jf, "ot")] = vt(v, v.tensor_scalar(
                            otb[:, :], s1b[:, :], m4b[:, 10:11], None, ALU.add))

                @block.sync
                def _(q, jfs=jfs):
                    for jf in jfs:
                        otb = s_ot[jf % 4]
                        W(q, tV[(jf, "ot")])
                        tD[(jf, "od")] = dma(q, dso[jf % 4],
                                             out_d[jf * P:(jf + 1) * P, :],
                                             otb[:, :])

            emit_stats(0)
            for j in range(NT):
                if j % CH2 == 0 and j // CH2 >= 2:
                    emit_finalize_chunk(j // CH2 - 2)
                emit_workA(j)
                if j + 1 < NT:
                    emit_stats(j + 1)
                if j >= 1:
                    emit_workB(j - 1)
            emit_workB(NT - 1)
            emit_finalize_chunk(NCH2 - 2)
            emit_finalize_chunk(NCH2 - 1)

            @block.gpsimd
            def _(g):
                for b in range(4):
                    g.wait_ge(dso[b].h, dso[b].n)

    return nc


def prep_inputs(xb, Wq, bq, Wk, bk, Wv, bv, Wg, bg, Wf, bf, pi):
    f32 = np.float32
    lo, hi = pi * H, (pi + 1) * H
    return {
        "xT": np.ascontiguousarray(xb.T).astype(f32, copy=False),
        "x": np.ascontiguousarray(xb),
        "xh": np.ascontiguousarray(xb[:, lo:hi]),
        "wq": np.ascontiguousarray(Wq.T[:, lo:hi]),
        "wk": np.ascontiguousarray(Wk.T[:, lo:hi]),
        "wv": np.ascontiguousarray(Wv.T[:, lo:hi]),
        "wg": np.ascontiguousarray(Wg.T[:, lo:hi]),
        "wf": np.ascontiguousarray(Wf.T[:, lo:hi]),
        "bq": np.ascontiguousarray(bq[lo:hi]).reshape(1, H),
        "bk": np.ascontiguousarray(bk[lo:hi]).reshape(1, H),
        "bv": np.ascontiguousarray(bv[lo:hi]).reshape(1, H),
        "bg": np.ascontiguousarray(bg[lo:hi]).reshape(1, H),
        "bf": np.ascontiguousarray(bf[lo:hi]).reshape(1, H),
        "tri": np.triu(np.ones((P, P), dtype=f32)),
        "id32": np.eye(P, dtype=f32),
        "idbf": np.eye(P, dtype=ml_dtypes.bfloat16),
        "onc": np.ones((P, 1), dtype=ml_dtypes.bfloat16),
        "onr": np.ones((1, P), dtype=f32),
    }


def kernel(L0_attn_x, L0_attn_Wq, L0_attn_bq, L0_attn_Wk, L0_attn_bk,
           L0_attn_Wv, L0_attn_bv, L0_attn_Wg, L0_attn_bg,
           L0_ffn_weight, L0_ffn_bias):
    x = np.asarray(L0_attn_x, dtype=np.float32)
    B, SEQ, Dm = x.shape
    assert Dm == D and B * 2 == 8
    nc = build_nc(SEQ)
    in_maps = []
    for c in range(8):
        p, pi = c // 2, c % 2
        in_maps.append(prep_inputs(
            x[p], np.asarray(L0_attn_Wq), np.asarray(L0_attn_bq),
            np.asarray(L0_attn_Wk), np.asarray(L0_attn_bk),
            np.asarray(L0_attn_Wv), np.asarray(L0_attn_bv),
            np.asarray(L0_attn_Wg), np.asarray(L0_attn_bg),
            np.asarray(L0_ffn_weight), np.asarray(L0_ffn_bias), pi))
    res = run_bass_kernel_spmd(nc, in_maps, core_ids=list(range(8)))
    out = np.empty((B, SEQ, D), dtype=np.float32)
    for c in range(8):
        p, pi = c // 2, c % 2
        out[p][:, pi * H:(pi + 1) * H] = res.results[c]["out"]
    return out


# version 14
# speedup vs baseline: 1.0562x; 1.0303x over previous
import numpy as np
import ml_dtypes
import concourse.bass as bass
import concourse.mybir as mybir
from concourse.bass_utils import run_bass_kernel_spmd
from contextlib import ExitStack

F32 = mybir.dt.float32
BF16 = mybir.dt.bfloat16
AF = mybir.ActivationFunctionType
ALU = mybir.AluOpType
AX = mybir.AxisListType.X

P = 128
D = 2048
H = 1024
KC = D // P
EPS = 1e-5
CH1 = 8
CH2 = 4
FRING = 10
ARING = 4
RG = [[0, 1], [2, 3], [4, 5], [6, 7]]


class Ctr:
    def __init__(self, h):
        self.h = h
        self.n = 0

    def inc(self, ins, by=1):
        ins.then_inc(self.h, by)
        self.n += by
        return (self, self.n)


class Waits:
    def __init__(self):
        self.last = {}

    def __call__(self, eng, tok):
        if tok is None:
            return
        ctr, val = tok
        if val is None or val <= 0:
            return
        key = (str(eng.engine), id(ctr))
        if self.last.get(key, 0) >= val:
            return
        eng.wait_ge(ctr.h, val)
        self.last[key] = val


def build_nc(SEQ, dbg=False):
    NT = SEQ // P
    NCH1 = NT // CH1
    NCH2 = NT // CH2
    RPC1 = CH1 * P
    RPC2 = CH2 * P

    nc = bass.Bass("TRN2", target_bir_lowering=False, debug=False)

    xT = nc.dram_tensor("xT", [D, SEQ], F32, kind="ExternalInput").ap()
    x_in = nc.dram_tensor("x", [SEQ, D], F32, kind="ExternalInput").ap()
    xh_in = nc.dram_tensor("xh", [SEQ, H], F32, kind="ExternalInput").ap()
    w_in = {nm: nc.dram_tensor(nm, [D, H], F32, kind="ExternalInput").ap()
            for nm in ("wq", "wk", "wv", "wg", "wf")}
    b_in = {nm: nc.dram_tensor(nm, [1, H], F32, kind="ExternalInput").ap()
            for nm in ("bq", "bk", "bv", "bg", "bf")}
    tri_in = nc.dram_tensor("tri", [P, P], F32, kind="ExternalInput").ap()
    id32_in = nc.dram_tensor("id32", [P, P], F32, kind="ExternalInput").ap()
    idbf_in = nc.dram_tensor("idbf", [P, P], BF16, kind="ExternalInput").ap()
    onc_in = nc.dram_tensor("onc", [P, 1], BF16, kind="ExternalInput").ap()
    onr_in = nc.dram_tensor("onr", [1, P], F32, kind="ExternalInput").ap()
    out_d = nc.dram_tensor("out", [SEQ, H], F32, kind="ExternalOutput").ap()

    asp = [nc.dram_tensor(f"asp{c}", [RPC1, H], F32) for c in range(NCH1)]
    agt = [nc.dram_tensor(f"agt{c}", [2 * RPC1, H], F32) for c in range(NCH1)]
    msp = [nc.dram_tensor(f"msp{c}", [RPC1, 4], F32) for c in range(NCH1)]
    mgt = [nc.dram_tensor(f"mgt{c}", [2 * RPC1, 4], F32) for c in range(NCH1)]
    mlc = nc.dram_tensor("mlc", [SEQ, 2], F32)
    m4sp = [nc.dram_tensor(f"m4sp{c}", [2 * RPC2, 1], F32) for c in range(NCH2)]
    m4gt = [nc.dram_tensor(f"m4gt{c}", [4 * RPC2, 1], F32) for c in range(NCH2)]

    W = Waits()

    with ExitStack() as top:
        E = top.enter_context
        s_tri = E(nc.sbuf_tensor("s_tri", [P, P], F32))
        s_id32 = E(nc.sbuf_tensor("s_id32", [P, P], F32))
        s_idbf = E(nc.sbuf_tensor("s_idbf", [P, P], BF16))
        s_onc = E(nc.sbuf_tensor("s_onc", [P, 1], BF16))
        s_onr = E(nc.sbuf_tensor("s_onr", [1, P], F32))
        s_eps = E(nc.sbuf_tensor("s_eps", [P, 1], F32))
        s_bb = E(nc.sbuf_tensor("s_bb", [1, H], BF16))
        s_ex = {nm: E(nc.sbuf_tensor(f"s_ex_{nm}", [2, H], BF16))
                for nm in ("q", "k", "v", "g", "f")}

        sV = Ctr(E(nc.semaphore("sV")))
        sS = Ctr(E(nc.semaphore("sS")))
        sT = Ctr(E(nc.semaphore("sT")))
        sG = Ctr(E(nc.semaphore("sG")))
        vb = Ctr(E(nc.semaphore("vb")))
        cca = Ctr(E(nc.semaphore("cca")))
        ccm = Ctr(E(nc.semaphore("ccm")))
        cc4 = Ctr(E(nc.semaphore("cc4")))
        dcn = Ctr(E(nc.semaphore("dcn")))
        dw = [Ctr(E(nc.semaphore(f"dw{b}"))) for b in range(2)]
        dxT = Ctr(E(nc.semaphore("dxT")))
        dx = Ctr(E(nc.semaphore("dx")))
        dxh = [Ctr(E(nc.semaphore(f"dxh{b}"))) for b in range(2)]
        dcy = Ctr(E(nc.semaphore("dcy")))
        dsp = [Ctr(E(nc.semaphore(f"dsp{b}"))) for b in range(ARING)]
        dmo = [Ctr(E(nc.semaphore(f"dmo{b}"))) for b in range(4)]
        dml = [Ctr(E(nc.semaphore(f"dml{b}"))) for b in range(2)]
        dx2 = [Ctr(E(nc.semaphore(f"dx2{b}"))) for b in range(2)]
        daf = [Ctr(E(nc.semaphore(f"daf{b}"))) for b in range(2)]
        dam = [Ctr(E(nc.semaphore(f"dam{b}"))) for b in range(2)]
        dm2 = [Ctr(E(nc.semaphore(f"dm2{b}"))) for b in range(2)]
        dml2 = [Ctr(E(nc.semaphore(f"dml2{b}"))) for b in range(2)]
        dmo4 = [Ctr(E(nc.semaphore(f"dmo4{b}"))) for b in range(4)]
        dmo4b = [Ctr(E(nc.semaphore(f"dmo4b{b}"))) for b in range(4)]
        dm4 = [Ctr(E(nc.semaphore(f"dm4{b}"))) for b in range(4)]
        dso = [Ctr(E(nc.semaphore(f"dso{b}"))) for b in range(4)]
        dal = Ctr(E(nc.semaphore("dal")))

        block = E(nc.Block())

        def dma(eng, ctr, dst, src):
            ins = eng.dma_start(out=dst, in_=src)
            return ctr.inc(ins, 16)

        def vt(v, ins):
            _, n = vb.inc(ins)
            v.wait_ge(vb.h, n)
            return (vb, n)

        with ExitStack() as ph1:
            E1 = ph1.enter_context
            s_w = {nm: E1(nc.sbuf_tensor(f"s_{nm}", [P, KC * H], BF16))
                   for nm in ("wq", "wk", "wv", "wg")}
            s_xt = E1(nc.sbuf_tensor("s_xt", [P, D], F32))
            s_xTf = E1(nc.sbuf_tensor("s_xTf", [P, D], F32))
            s_xTb = [E1(nc.sbuf_tensor(f"s_xTb{b}", [P, D], BF16)) for b in range(2)]
            s_xhp = [E1(nc.sbuf_tensor(f"s_xhp{b}", [P, H], F32)) for b in range(2)]
            s_ks = E1(nc.sbuf_tensor("s_ks", [P, H], F32))
            s_gs = E1(nc.sbuf_tensor("s_gs", [P, H], F32))
            s_kv = E1(nc.sbuf_tensor("s_kv", [P, H], F32))
            s_qg = s_ks
            s_at = [E1(nc.sbuf_tensor(f"s_at{b}", [P, H], F32)) for b in range(ARING)]
            s_mo = [E1(nc.sbuf_tensor(f"s_mo{b}", [P, 4], F32)) for b in range(4)]
            s_cy = E1(nc.sbuf_tensor("s_cy", [1, H], F32))
            s_csb = s_kv
            s_st = [E1(nc.sbuf_tensor(f"s_st{b}", [P, 8], F32)) for b in range(2)]
            s_mc = [E1(nc.sbuf_tensor(f"s_mc{b}", [P, 2], F32)) for b in range(2)]
            s_mr = [E1(nc.sbuf_tensor(f"s_mr{b}", [2, P], BF16)) for b in range(2)]
            ps = E1(nc.psum_tensor("ps", [P, 4096], F32))
            pv = ps[:, 0:H]
            pq = ps[:, H:2 * H]
            pk = ps[:, 2 * H:3 * H]
            pg = ps[:, 3 * H:4 * H]
            pcs = pk

            @block.sync
            def _(s):
                for src, dst in ((tri_in, s_tri[:, :]), (id32_in, s_id32[:, :]),
                                 (idbf_in, s_idbf[:, :]), (onc_in, s_onc[:, :]),
                                 (onr_in, s_onr[:, :])):
                    dma(s, dcn, dst, src)

            cast_tok = {}
            rowdma = {}
            bias_hold = [None]
            for bi, nm in enumerate(("bq", "bk", "bv", "bg")):
                @block.sync
                def _(s, nm=nm, bi=bi):
                    if bi > 0:
                        W(s, rowdma[bi - 1])
                        W(s, cast_tok[bi - 1])
                    bias_hold[0] = dma(s, dcn, s_xTf[0:1, 0:H], b_in[nm])

                @block.vector
                def _(v, nm=nm, bi=bi):
                    if bi == 0:
                        v.memset(s_eps[:, :], EPS)
                        for b in range(4):
                            v.memset(s_mo[b][:, :], 0.0)
                    W(v, bias_hold[0])
                    cast_tok[bi] = sV.inc(
                        v.tensor_copy(s_bb[0:1, :], s_xTf[0:1, 0:H]))

                @block.sync
                def _(s, nm=nm, bi=bi):
                    W(s, cast_tok[bi])
                    rowdma[bi] = dma(s, dcn, s_ex[nm[1]][1:2, :], s_bb[0:1, :])
            consts_all = (dcn, dcn.n)

            wcast_tok = {}
            wld_hold = [None]
            for idx in range(4 * KC):
                nm = ("wq", "wk", "wv", "wg")[idx // KC]
                k = idx % KC
                slot = idx % 2
                stg = s_xt[:, slot * H:(slot + 1) * H]

                @block.sync
                def _(s, nm=nm, k=k, stg=stg, idx=idx, slot=slot):
                    if idx >= 2:
                        W(s, wcast_tok[idx - 2])
                    wld_hold[0] = dma(s, dw[slot], stg,
                                      w_in[nm][k * P:(k + 1) * P, :])

                @block.vector
                def _(v, nm=nm, k=k, stg=stg, idx=idx):
                    W(v, wld_hold[0])
                    wcast_tok[idx] = sV.inc(
                        v.tensor_copy(s_w[nm][:, k * H:(k + 1) * H], stg))

            sw_ps = {"q": ps[0:1, H:2 * H], "k": ps[0:1, 2 * H:3 * H],
                     "v": ps[0:1, 0:H], "g": ps[0:1, 3 * H:4 * H]}
            sw_mm_tok = {}
            for wi, nm in enumerate(("wq", "wk", "wv", "wg")):
                @block.tensor
                def _(t, nm=nm, wi=wi):
                    W(t, wcast_tok[wi * KC + KC - 1])
                    W(t, consts_all)
                    ins = None
                    for k in range(KC):
                        for n in range(2):
                            ins = t.matmul(
                                sw_ps[nm[1]][:, n * 512:(n + 1) * 512], s_onc[:, :],
                                s_w[nm][:, k * H + n * 512:k * H + (n + 1) * 512],
                                start=(k == 0), stop=(k == KC - 1))
                    sw_mm_tok[nm[1]] = sT.inc(ins)

            setup_done = [None]

            @block.vector
            def _(v):
                tok = None
                for nm in ("q", "k", "v", "g"):
                    W(v, sw_mm_tok[nm])
                    tok = sV.inc(v.tensor_copy(s_ex[nm][0:1, :], sw_ps[nm]))
                setup_done[0] = tok
            setup_v_tok = setup_done[0]

            tokV, tokS, tokT, tokD = {}, {}, {}, {}

            def proj(t, xtb, wsb, pdst):
                for k in range(KC):
                    for n in range(2):
                        t.matmul(
                            pdst[:, n * 512:(n + 1) * 512],
                            xtb[:, k * P:(k + 1) * P],
                            wsb[:, k * H + n * 512:k * H + (n + 1) * 512],
                            start=(k == 0), stop=False)

            def proj_close(t, pdst, exname, mr):
                ins = None
                for n in range(2):
                    ins = t.matmul(pdst[:, n * 512:(n + 1) * 512], mr[:, :],
                                   s_ex[exname][:, n * 512:(n + 1) * 512],
                                   start=False, stop=True)
                return ins

            def emit_front(j):
                stj = s_st[j % 2]
                mcj = s_mc[j % 2]
                xtb = s_xTb[j % 2]

                @block.vector
                def _(v, j=j, stj=stj, xtb=xtb):
                    if j == 0:
                        W(v, setup_v_tok)
                    W(v, tokD[(j, "xT")])
                    if j >= 2:
                        W(v, tokT[(j - 2, "g")])
                    tokV[(j, "cast")] = sV.inc(v.tensor_copy(xtb[:, :], s_xTf[:, :]))
                    W(v, tokD[(j, "x")])
                    if j >= 2:
                        W(v, tokD[(j - 2, "ml")])
                    vt(v, v.reduce_sum(stj[:, 0:1], s_xt[:, :], axis=AX))
                    vt(v, v.tensor_scalar(stj[:, 2:3], stj[:, 0:1], 1.0 / D, None,
                                          ALU.mult))
                    vt(v, v.tensor_tensor(stj[:, 3:4], stj[:, 2:3], stj[:, 2:3],
                                          ALU.mult))
                    vt(v, v.scalar_tensor_tensor(s_xTf[:, :], s_xt[:, :], 1.0,
                                                 s_xt[:, :], ALU.mult, ALU.mult,
                                                 accum_out=stj[:, 1:2]))
                    tokV[(j, "var")] = sV.inc(
                        v.tensor_scalar(stj[:, 4:5], stj[:, 1:2], 1.0 / D,
                                        stj[:, 3:4], ALU.mult, ALU.subtract))

                @block.scalar
                def _(s, j=j, stj=stj, mcj=mcj):
                    W(s, tokV[(j, "var")])
                    if j >= 2:
                        W(s, tokT[(j - 2, "tr")])
                    tokS[(j, "std")] = sS.inc(
                        s.activation(mcj[:, 1:2], stj[:, 4:5], AF.Sqrt,
                                     bias=s_eps[:, :]))

                @block.vector
                def _(v, j=j, stj=stj, mcj=mcj):
                    W(v, tokS[(j, "std")])
                    vt(v, v.reciprocal(stj[:, 5:6], mcj[:, 1:2]))
                    vt(v, v.tensor_tensor(stj[:, 6:7], stj[:, 5:6], stj[:, 5:6],
                                          ALU.mult))
                    tokV[(j, "stat")] = sV.inc(
                        v.tensor_scalar(mcj[:, 0:1], stj[:, 2:3], -1.0, None,
                                        ALU.mult))

            @block.sync
            def _(s):
                W(s, rowdma[3])
                W(s, cast_tok[3])
                src = xT.rearrange("(k p) q -> p k q", p=P)[:, :, 0:P]
                dst = s_xTf[:, :].rearrange("p (k c) -> p k c", k=KC)
                tokD[(0, "xT")] = dma(s, dxT, dst, src)
                W(s, wcast_tok[4 * KC - 1])
                tokD[(0, "x")] = dma(s, dx, s_xt[:, :], x_in[0:P, :])
                tokD[(0, "xh")] = dma(s, dxh[0], s_xhp[0][:, :], xh_in[0:P, :])
            emit_front(0)

            for i in range(NT):
                xtb = s_xTb[i % 2]
                atb = s_at[i % ARING]
                mob = s_mo[i % 4]
                xhb = s_xhp[i % 2]
                sti = s_st[i % 2]
                mri = s_mr[i % 2]
                rstd = sti[:, 5:6]
                rstd2 = sti[:, 6:7]

                if i + 1 < NT:
                    @block.sync
                    def _(s, i=i):
                        j = i + 1
                        W(s, tokV[(i, "var")])
                        src = xT.rearrange("(k p) q -> p k q",
                                           p=P)[:, :, j * P:(j + 1) * P]
                        dst = s_xTf[:, :].rearrange("p (k c) -> p k c", k=KC)
                        tokD[(j, "xT")] = dma(s, dxT, dst, src)
                        W(s, tokV[(i, "var")])
                        tokD[(j, "x")] = dma(s, dx, s_xt[:, :],
                                             x_in[j * P:(j + 1) * P, :])
                        if j >= 2:
                            W(s, tokV[(j - 2, "cross")])
                        tokD[(j, "xh")] = dma(s, dxh[j % 2], s_xhp[j % 2][:, :],
                                              xh_in[j * P:(j + 1) * P, :])

                @block.tensor
                def _(t, i=i, xtb=xtb, mri=mri):
                    if i == 0:
                        W(t, setup_v_tok)
                        W(t, consts_all)
                    W(t, tokV[(i, "cast")])
                    if i > 0:
                        W(t, tokV[(i - 1, "kv")])
                    proj(t, xtb, s_w["wv"], pv)
                    W(t, tokV[(i, "stat")])
                    if i > 0:
                        W(t, tokS[(i - 1, "gs")])
                    ins = t.transpose(ps[0:2, 3 * H:3 * H + P], s_mc[i % 2][:, :],
                                      s_id32[:, :])
                    tokT[(i, "tr")] = sT.inc(ins)

                @block.vector
                def _(v, i=i, mri=mri):
                    W(v, tokT[(i, "tr")])
                    if i >= 2:
                        W(v, tokT[(i - 2, "g")])
                    tokV[(i, "mr")] = sV.inc(
                        v.tensor_copy(mri[:, :], ps[0:2, 3 * H:3 * H + P]))

                if i + 1 < NT:
                    emit_front(i + 1)

                @block.tensor
                def _(t, i=i, xtb=xtb, mri=mri):
                    if i > 0:
                        W(t, tokV[(i - 1, "qg")])
                    proj(t, xtb, s_w["wq"], pq)
                    W(t, tokV[(i, "mr")])
                    tokT[(i, "v")] = sT.inc(proj_close(t, pv, "v", mri))
                    tokT[(i, "q")] = sT.inc(proj_close(t, pq, "q", mri))
                    if i > 0:
                        W(t, tokV[(i - 1, "csb")])
                    proj(t, xtb, s_w["wk"], pk)
                    tokT[(i, "k")] = sT.inc(proj_close(t, pk, "k", mri))
                    proj(t, xtb, s_w["wg"], pg)
                    tokT[(i, "g")] = sT.inc(proj_close(t, pg, "g", mri))

                @block.scalar
                def _(s, i=i, rstd=rstd):
                    W(s, tokT[(i, "g")])
                    if i > 0:
                        W(s, tokV[(i - 1, "qg")])
                    tokS[(i, "gs")] = sS.inc(
                        s.activation(s_gs[:, :], pg[:, :], AF.Sigmoid, scale=rstd))

                @block.vector
                def _(v, i=i, rstd2=rstd2):
                    W(v, tokT[(i, "k")])
                    tokV[(i, "ks")] = vt(v, v.tensor_scalar(
                        s_ks[:, :], pk[:, :], rstd2, None, ALU.mult))
                    W(v, tokT[(i, "v")])
                    if i > 0:
                        W(v, tokT[(i - 1, "cs")])
                        W(v, tokD[(i - 1, "cy")])
                    tokV[(i, "kv")] = vt(v, v.tensor_tensor(
                        s_kv[:, :], s_ks[:, :], pv[:, :], ALU.mult))

                @block.tensor
                def _(t, i=i):
                    W(t, tokV[(i, "kv")])
                    W(t, tokV[(i, "ks")])
                    if i > 0:
                        W(t, tokD[(i - 1, "cy")])
                    ins = None
                    for n in range(2):
                        ins = t.matmul(pcs[:, n * 512:(n + 1) * 512], s_tri[:, :],
                                       s_kv[:, n * 512:(n + 1) * 512],
                                       start=True, stop=(i == 0))
                        if i > 0:
                            ins = t.matmul(pcs[:, n * 512:(n + 1) * 512],
                                           s_onr[:, :],
                                           s_cy[:, n * 512:(n + 1) * 512],
                                           start=False, stop=True)
                    tokT[(i, "cs")] = sT.inc(ins)

                @block.vector
                def _(v, i=i, atb=atb, mob=mob, xhb=xhb, rstd=rstd):
                    W(v, tokT[(i, "q")])
                    W(v, tokS[(i, "gs")])
                    tokV[(i, "qg")] = vt(v, v.scalar_tensor_tensor(
                        s_qg[:, :], pq[:, :], rstd, s_gs[:, :],
                        ALU.mult, ALU.mult))
                    W(v, tokT[(i, "cs")])
                    if i >= ARING:
                        W(v, tokD[(i - ARING, "at")])
                    tokV[(i, "attn")] = vt(v, v.tensor_tensor(
                        atb[:, :], s_qg[:, :], pcs[:, :], ALU.mult))
                    if i >= 4:
                        W(v, tokD[(i - 4, "mo")])
                    sV.inc(v.reduce_sum(mob[:, 0:1], atb[:, :], axis=AX))
                    W(v, tokD[(i, "xh")])
                    tokV[(i, "cross")] = vt(v, v.scalar_tensor_tensor(
                        s_kv[:, :], xhb[:, :], 1.0, atb[:, :], ALU.mult, ALU.mult,
                        accum_out=mob[:, 2:3]))
                    tokV[(i, "asq")] = vt(v, v.scalar_tensor_tensor(
                        s_kv[:, :], atb[:, :], 1.0, atb[:, :], ALU.mult, ALU.mult,
                        accum_out=mob[:, 1:2]))
                    if i > 0:
                        W(v, tokD[(i - 1, "cy")])
                    tokV[(i, "csb")] = vt(v, v.tensor_copy(
                        s_csb[:, :], pcs[:, :]))

                @block.gpsimd
                def _(g, i=i, atb=atb, mob=mob, sti=sti):
                    c = i // CH1
                    r = (i % CH1) * P
                    W(g, tokV[(i, "csb")])
                    tokD[(i, "cy")] = dma(g, dcy, s_cy[:, :], s_csb[P - 1:P, :])
                    W(g, tokV[(i, "attn")])
                    tokD[(i, "at")] = dma(g, dsp[i % ARING],
                                          asp[c].ap()[r:r + P, :], atb[:, :])
                    W(g, tokV[(i, "asq")])
                    tokD[(i, "mo")] = dma(g, dmo[i % 4],
                                          msp[c].ap()[r:r + P, :], mob[:, :])
                    tokD[(i, "ml")] = dma(g, dml[i % 2],
                                          mlc.ap()[i * P:(i + 1) * P, :],
                                          sti[:, 0:2])
                    if i % CH1 == CH1 - 1:
                        for b in range(ARING):
                            W(g, tokD[(i - ((i - b) % ARING), "at")])
                        for b in range(4):
                            W(g, tokD[(i - ((i - b) % 4), "mo")])
                        cca.inc(g.collective_compute(
                            "AllGather", ALU.bypass, replica_groups=RG,
                            ins=[asp[c].ap().opt()], outs=[agt[c].ap().opt()]))
                        ccm.inc(g.collective_compute(
                            "AllGather", ALU.bypass, replica_groups=RG,
                            ins=[msp[c].ap().opt()], outs=[mgt[c].ap().opt()]))

            ph1_end = dict(v=(sV, sV.n), s=(sS, sS.n), t=(sT, sT.n))
            ph1_mltok = {i: tokD[(i, "ml")] for i in range(NT)}
            ph1_dmas = [(c, c.n) for c in
                        ([dcy, dxT, dx, dcn] + dw + dxh + dsp + dmo + dml)]

        with ExitStack() as ph2:
            E2 = ph2.enter_context
            s_wf = E2(nc.sbuf_tensor("s_wf", [P, KC * H], BF16))
            s_x2 = [E2(nc.sbuf_tensor(f"s_x2_{b}", [P, D], F32)) for b in range(1)]
            s_af = [E2(nc.sbuf_tensor(f"s_af{b}", [P, D], F32)) for b in range(1)]
            s_am = [E2(nc.sbuf_tensor(f"s_am{b}", [P, H], F32)) for b in range(2)]
            s_xh = [E2(nc.sbuf_tensor(f"s_xh{b}", [P, H], F32)) for b in range(1)]
            s_l2 = E2(nc.sbuf_tensor("s_l2", [P, D], F32))
            s_l2m = E2(nc.sbuf_tensor("s_l2m", [P, H], F32))
            s_yb = [E2(nc.sbuf_tensor(f"s_yb{b}", [P, D], BF16)) for b in range(2)]
            s_yT = [E2(nc.sbuf_tensor(f"s_yT{b}", [P, D], BF16)) for b in range(2)]
            s_fr = [E2(nc.sbuf_tensor(f"s_fr{b}", [P, H], F32))
                    for b in range(FRING)]
            s_s1 = [E2(nc.sbuf_tensor(f"s_s1_{b}", [P, H], F32))
                    for b in range(FRING)]
            s_ot = [E2(nc.sbuf_tensor(f"s_ot{b}", [P, H], F32)) for b in range(4)]
            s_m2 = [E2(nc.sbuf_tensor(f"s_m2_{b}", [P, 8], F32)) for b in range(2)]
            s_mlb = [E2(nc.sbuf_tensor(f"s_mlb{b}", [P, 2], F32)) for b in range(2)]
            s_m4 = [E2(nc.sbuf_tensor(f"s_m4_{b}", [P, 12], F32)) for b in range(4)]
            s_mo4a = [E2(nc.sbuf_tensor(f"s_mo4a{b}", [P, 1], F32)) for b in range(4)]
            s_mo4b = [E2(nc.sbuf_tensor(f"s_mo4b{b}", [P, 1], F32)) for b in range(4)]
            s_st2 = [E2(nc.sbuf_tensor(f"s_st2_{b}", [P, 24], F32))
                     for b in range(2)]
            s_mc2 = [E2(nc.sbuf_tensor(f"s_mc2_{b}", [P, 2], F32))
                     for b in range(2)]
            s_mcb = [E2(nc.sbuf_tensor(f"s_mcb{b}", [P, 2], BF16))
                     for b in range(2)]
            s_mr2 = [E2(nc.sbuf_tensor(f"s_mr2{i}", [2, P], BF16)) for i in range(2)]
            s_dmp = E2(nc.sbuf_tensor("s_dmp", [P, H], F32))
            s_dsc = E2(nc.sbuf_tensor("s_dsc", [P, 128], F32))
            s_stg = [s_x2[0][:, 0:H], s_x2[0][:, H:D]]
            s_abs = [s_yT[0][:, 0:H], s_yT[0][:, H:D]]
            s_alr = s_l2[0:1, 0:H]
            s_bfs = s_alr
            s_alb = s_yb[0][0:1, 0:H]
            s_alf = s_yb[0][:, H:D]
            s_ssr = s_l2m[0:1, 0:H]
            ps2 = E2(nc.psum_tensor("ps2", [P, 2048], F32))
            psb = E2(nc.psum_tensor("psb", [P, 2176], BF16))

            def bwait(eng):
                W(eng, ph1_end["t"])
                W(eng, ph1_end["v"])
                W(eng, ph1_end["s"])
                for tok in ph1_dmas:
                    W(eng, tok)

            wfl_hold = [None]
            abs_tok, sgn_tok = {}, {}
            for k in range(KC):
                slot = k % 2

                @block.sync
                def _(s, k=k, slot=slot):
                    bwait(s)
                    if k >= 2:
                        W(s, sgn_tok[k - 2])
                    wfl_hold[0] = dma(s, dw[slot], s_stg[slot],
                                      w_in["wf"][k * P:(k + 1) * P, :])

                @block.scalar
                def _(s, k=k, slot=slot):
                    bwait(s)
                    W(s, wfl_hold[0])
                    if k >= 2:
                        W(s, abs_tok[k - 2])
                    sS.inc(s.activation(s_wf[:, k * H:(k + 1) * H], s_stg[slot],
                                        AF.Sign))
                    sgn_tok[k] = sS.inc(
                        s.activation(s_abs[slot][:, :], s_stg[slot], AF.Abs))

                @block.tensor
                def _(t, k=k, slot=slot):
                    bwait(t)
                    W(t, sgn_tok[k])
                    ins = None
                    for n in range(2):
                        ins = t.matmul(ps2[0:1, n * 512:(n + 1) * 512], s_onc[:, :],
                                       s_abs[slot][:, n * 512:(n + 1) * 512],
                                       start=(k == 0), stop=(k == KC - 1))
                    abs_tok[k] = sT.inc(ins)

            ss_tok = [None]

            @block.tensor
            def _(t):
                W(t, sgn_tok[KC - 1])
                ins = None
                for k in range(KC):
                    for n in range(2):
                        ins = t.matmul(ps2[0:1, H + n * 512:H + (n + 1) * 512],
                                       s_onc[:, :],
                                       s_wf[:, k * H + n * 512:k * H + (n + 1) * 512],
                                       start=(k == 0), stop=(k == KC - 1))
                ss_tok[0] = sT.inc(ins)

            alb_tok = [None]

            @block.vector
            def _(v):
                bwait(v)
                W(v, abs_tok[KC - 1])
                vt(v, v.tensor_scalar(s_alr, ps2[0:1, 0:H], 1.0 / D, None, ALU.mult))
                alb_tok[0] = sV.inc(v.tensor_copy(s_alb, s_alr))

            alf_tok = [None]

            @block.gpsimd
            def _(g):
                bwait(g)
                W(g, alb_tok[0])
                tok = dma(g, dal, s_alf[0:1, :], s_alb)
                rows = 1
                while rows < P:
                    W(g, tok)
                    tok = dma(g, dal, s_alf[rows:2 * rows, :], s_alf[0:rows, :])
                    rows *= 2
                alf_tok[0] = tok

            wfa_tok = {}
            for k in range(KC):
                @block.vector
                def _(v, k=k):
                    W(v, alf_tok[0])
                    W(v, ss_tok[0])
                    wfa_tok[k] = sV.inc(
                        v.tensor_tensor(s_wf[:, k * H:(k + 1) * H],
                                        s_wf[:, k * H:(k + 1) * H],
                                        s_alf[:, :], ALU.mult))

            sbwrow_tok = [None]

            @block.vector
            def _(v):
                W(v, ss_tok[0])
                vt(v, v.tensor_copy(s_ssr, ps2[0:1, H:2 * H]))
                vt(v, v.tensor_tensor(s_ssr, s_ssr, s_alr, ALU.mult))
                sbwrow_tok[0] = sV.inc(v.tensor_copy(s_ex["f"][0:1, :], s_ssr))

            bfl = [None]

            @block.sync
            def _(s):
                W(s, sbwrow_tok[0])
                bfl[0] = dma(s, dcn, s_bfs, b_in["bf"])

            exf_cast = [None]

            @block.vector
            def _(v):
                W(v, bfl[0])
                exf_cast[0] = sV.inc(v.tensor_copy(s_bb[0:1, :], s_bfs))

            exf_row = [None]

            @block.sync
            def _(s):
                W(s, exf_cast[0])
                exf_row[0] = dma(s, dcn, s_ex["f"][1:2, :], s_bb[0:1, :])

            tV, tS, tT, tD, tG = {}, {}, {}, {}, {}

            def emit_stats(j):
                c1 = j // CH1
                r1 = (j % CH1) * P
                c2 = j // CH2
                x2b = s_x2[0]
                afb = s_af[0]
                amb = s_am[j % 2]
                xhb = s_xh[0]
                m2b = s_m2[j % 2]
                mlb = s_mlb[j % 2]
                frb = s_fr[j % FRING]
                s1b = s_s1[j % FRING]
                moa = s_mo4a[j % 4]
                mosb = s_mo4b[j % 4]
                pf = ps2[:, (j % 2) * H:(j % 2 + 1) * H]
                st = s_st2[j % 2]
                mc2 = s_mc2[j % 2]
                mcb = s_mcb[j % 2]

                @block.sync
                def _(s, j=j, c1=c1, r1=r1, x2b=x2b, afb=afb, amb=amb, xhb=xhb,
                      m2b=m2b, mlb=mlb):
                    if j < 2:
                        W(s, sgn_tok[KC - 1])
                        W(s, exf_row[0])
                    if j >= 1:
                        W(s, tG[(j - 1, "y")])
                    tD[(j, "xa")] = dma(s, dx2[1], x2b[:, 0:H],
                                        x_in[j * P:(j + 1) * P, 0:H])
                    tD[(j, "x")] = dma(s, dx2[0], x2b[:, H:D],
                                       x_in[j * P:(j + 1) * P, H:D])
                    W(s, (cca, c1 + 1))
                    if j >= 1:
                        W(s, tV[(j - 1, "l2")])
                    tD[(j, "afa")] = dma(s, daf[1], afb[:, 0:H],
                                         agt[c1].ap()[r1:r1 + P, :])
                    tD[(j, "af")] = dma(s, daf[0], afb[:, H:D],
                                        agt[c1].ap()[RPC1 + r1:RPC1 + r1 + P, :])


                @block.scalar
                def _(s, j=j, c1=c1, r1=r1, m2b=m2b, mlb=mlb):
                    if j == 0:
                        bwait(s)
                    W(s, (ccm, c1 + 1))
                    if j >= 2:
                        W(s, tV[(j - 2, "m2c")])
                    dma(s, dm2[j % 2], m2b[:, 0:4], mgt[c1].ap()[r1:r1 + P, :])
                    tD[(j, "m2")] = dma(s, dm2[j % 2], m2b[:, 4:8],
                                        mgt[c1].ap()[RPC1 + r1:RPC1 + r1 + P, :])
                    if j >= 2:
                        W(s, tV[(j - 2, "alg")])
                    W(s, ph1_mltok[j])
                    tD[(j, "ml")] = dma(s, dml2[j % 2], mlb[:, :],
                                        mlc.ap()[j * P:(j + 1) * P, :])

                @block.vector
                def _(v, j=j, m2b=m2b, st=st):
                    if j == 0:
                        bwait(v)
                    W(v, tD[(j, "m2")])
                    vt(v, v.tensor_tensor(st[:, 0:4], m2b[:, 0:4], m2b[:, 4:8],
                                          ALU.add))
                    vt(v, v.tensor_scalar(st[:, 4:5], st[:, 0:1], 1.0 / D, None,
                                          ALU.mult))
                    vt(v, v.tensor_tensor(st[:, 5:6], st[:, 4:5], st[:, 4:5],
                                          ALU.mult))
                    tV[(j, "m2c")] = sV.inc(
                        v.tensor_scalar(st[:, 6:7], st[:, 1:2], 1.0 / D,
                                        st[:, 5:6], ALU.mult, ALU.subtract))

                @block.scalar
                def _(s, j=j, st=st, c1=c1, r1=r1, amb=amb):
                    if j == 0:
                        bwait(s)
                    s.activation(s_dsc[:, (2 * j) % 128:(2 * j) % 128 + 1],
                                 s_eps[:, :], AF.Sqrt, bias=s_eps[:, :])
                    W(s, tV[(j, "m2c")])
                    tS[(j, "std2")] = sS.inc(
                        s.activation(st[:, 7:8], st[:, 6:7], AF.Sqrt,
                                     bias=s_eps[:, :]))
                    if j >= 2:
                        W(s, tV[(j - 2, "l2m")])
                    tD[(j, "am")] = dma(s, dam[j % 2], amb[:, :],
                                        asp[c1].ap()[r1:r1 + P, :])

                @block.vector
                def _(v, j=j, st=st, afb=afb, amb=amb):
                    W(v, tS[(j, "std2")])
                    vt(v, v.reciprocal(st[:, 8:9], st[:, 7:8]))
                    tV[(j, "r2")] = vt(v, v.tensor_scalar(
                        st[:, 9:10], st[:, 4:5], st[:, 8:9], -1.0,
                        ALU.mult, ALU.mult))
                    W(v, tD[(j, "afa")])
                    if j >= 1:
                        W(v, tG[(j - 1, "y")])
                    tV[(j, "l2a")] = sV.inc(
                        v.tensor_scalar(s_l2[:, 0:H], afb[:, 0:H], st[:, 8:9],
                                        st[:, 9:10], ALU.mult, ALU.add))
                    W(v, tD[(j, "af")])
                    tV[(j, "l2")] = sV.inc(
                        v.tensor_scalar(s_l2[:, H:D], afb[:, H:D], st[:, 8:9],
                                        st[:, 9:10], ALU.mult, ALU.add))
                    W(v, tD[(j, "am")])
                    if j >= 1:
                        W(v, tG[(j - 1, "s1")])
                    tV[(j, "l2m")] = sV.inc(
                        v.tensor_scalar(s_l2m[:, :], amb[:, :], st[:, 8:9],
                                        st[:, 9:10], ALU.mult, ALU.add))

                @block.vector
                def _(v, j=j, st=st, mlb=mlb):
                    W(v, tD[(j, "ml")])
                    W(v, tV[(j, "r2")])
                    v.tensor_scalar(st[:, 10:11], mlb[:, 0:1], 1.0 / D, None,
                                    ALU.mult)
                    v.tensor_scalar(st[:, 11:12], mlb[:, 0:1], st[:, 4:5],
                                    st[:, 2:3], ALU.mult, ALU.subtract)
                    v.tensor_scalar(st[:, 13:14], st[:, 5:6], float(D), None,
                                    ALU.mult)
                    vt(v, v.tensor_tensor(st[:, 15:16], st[:, 8:9], st[:, 8:9],
                                          ALU.mult))
                    v.tensor_scalar(st[:, 12:13], st[:, 11:12], st[:, 8:9],
                                    -2.0, ALU.mult, ALU.mult)
                    v.tensor_scalar(st[:, 14:15], st[:, 1:2], 1.0,
                                    st[:, 13:14], ALU.mult, ALU.subtract)
                    vt(v, v.tensor_tensor(st[:, 18:19], st[:, 10:11], st[:, 10:11],
                                          ALU.mult))
                    v.tensor_tensor(st[:, 16:17], st[:, 14:15], st[:, 15:16],
                                    ALU.mult)
                    vt(v, v.tensor_tensor(st[:, 17:18], mlb[:, 1:2], st[:, 12:13],
                                          ALU.add))
                    vt(v, v.tensor_tensor(st[:, 17:18], st[:, 17:18], st[:, 16:17],
                                          ALU.add))
                    tV[(j, "alg")] = sV.inc(
                        v.tensor_scalar(st[:, 19:20], st[:, 17:18], 1.0 / D,
                                        st[:, 18:19], ALU.mult, ALU.subtract))

                @block.scalar
                def _(s, j=j, st=st, xhb=xhb):
                    s.activation(s_dsc[:, (2 * j + 1) % 128:(2 * j + 1) % 128 + 1],
                                 s_eps[:, :], AF.Sqrt, bias=s_eps[:, :])
                    W(s, tV[(j, "alg")])
                    tS[(j, "std3")] = sS.inc(
                        s.activation(st[:, 20:21], st[:, 19:20], AF.Sqrt,
                                     bias=s_eps[:, :]))
                    if j >= 1:
                        W(s, tG[(j - 1, "s1")])
                    tD[(j, "xh")] = dma(s, dxh[0], xhb[:, :],
                                        xh_in[j * P:(j + 1) * P, :])

                @block.vector
                def _(v, j=j, st=st):
                    W(v, tS[(j, "std3")])
                    vt(v, v.reciprocal(st[:, 21:22], st[:, 20:21]))
                    vt(v, v.tensor_scalar(st[:, 22:23], st[:, 10:11], -1.0, None,
                                          ALU.mult))
                    vt(v, v.tensor_copy(mc2[:, 0:1], st[:, 22:23]))
                    vt(v, v.tensor_copy(mc2[:, 1:2], st[:, 20:21]))
                    if j >= 2:
                        W(v, tT[(j - 2, "trm")])
                    tV[(j, "mcb")] = sV.inc(v.tensor_copy(mcb[:, :], mc2[:, :]))

                @block.gpsimd
                def _(g, j=j, x2b=x2b, s1b=s1b, xhb=xhb):
                    if j == 0:
                        bwait(g)
                        W(g, sbwrow_tok[0])
                    W(g, tV[(j, "l2a")])
                    W(g, tD[(j, "xa")])
                    if j >= 2:
                        W(g, tT[(j - 2, "tr")])
                    if j <= 1:
                        W(g, alf_tok[0])
                    tG[(j, "ya")] = sG.inc(
                        g.tensor_tensor(s_yb[j % 2][:, 0:H], x2b[:, 0:H],
                                        s_l2[:, 0:H], ALU.add))
                    W(g, tV[(j, "l2")])
                    W(g, tD[(j, "x")])
                    tG[(j, "y")] = sG.inc(
                        g.tensor_tensor(s_yb[j % 2][:, H:D], x2b[:, H:D],
                                        s_l2[:, H:D], ALU.add))
                    W(g, tV[(j, "l2m")])
                    W(g, tD[(j, "xh")])
                    if j >= FRING:
                        W(g, tV[(j - FRING, "ot")])
                    tG[(j, "s1")] = sG.inc(
                        g.tensor_tensor(s1b[:, :], xhb[:, :], s_l2m[:, :], ALU.add))


            def emit_workA(j):
                c1 = j // CH1
                r1 = (j % CH1) * P
                c2 = j // CH2
                x2b = s_x2[0]
                afb = s_af[0]
                amb = s_am[j % 2]
                xhb = s_xh[0]
                m2b = s_m2[j % 2]
                mlb = s_mlb[j % 2]
                frb = s_fr[j % FRING]
                s1b = s_s1[j % FRING]
                moa = s_mo4a[j % 4]
                mosb = s_mo4b[j % 4]
                pf = ps2[:, (j % 2) * H:(j % 2 + 1) * H]
                st = s_st2[j % 2]
                mc2 = s_mc2[j % 2]
                mcb = s_mcb[j % 2]

                @block.tensor
                def _(t, j=j):
                    W(t, tG[(j, "ya")])
                    if j == 0:
                        W(t, ss_tok[0])
                    if j >= 1:
                        W(t, tV[(j - 1, "yTea")])
                    ins = None
                    for k in range(KC // 2):
                        ins = t.transpose(psb[:, k * P:(k + 1) * P],
                                          s_yb[j % 2][:, k * P:(k + 1) * P],
                                          s_idbf[:, :])
                    tT[(j, "tra")] = sT.inc(ins)
                    W(t, tG[(j, "y")])
                    if j >= 1:
                        W(t, tV[(j - 1, "yTe")])
                    for k in range(KC // 2, KC):
                        ins = t.transpose(psb[:, k * P:(k + 1) * P],
                                          s_yb[j % 2][:, k * P:(k + 1) * P],
                                          s_idbf[:, :])
                    tT[(j, "tr")] = sT.inc(ins)
                    W(t, tV[(j, "mcb")])
                    ins = t.transpose(psb[0:2, 2048:2176], s_mcb[j % 2][:, :],
                                      s_idbf[:, :])
                    tT[(j, "trm")] = sT.inc(ins)

                @block.vector
                def _(v, j=j):
                    W(v, tT[(j, "tra")])
                    if j <= 1:
                        for k in range(KC):
                            W(v, wfa_tok[k])
                        W(v, ss_tok[0])
                    if j >= 2:
                        W(v, tT[(j - 2, "f")])
                    tV[(j, "yTea")] = sV.inc(
                        v.tensor_copy(s_yT[j % 2][:, 0:H], psb[:, 0:H]))
                    W(v, tT[(j, "tr")])
                    tV[(j, "yTe")] = sV.inc(
                        v.tensor_copy(s_yT[j % 2][:, H:D], psb[:, H:D]))
                    W(v, tT[(j, "trm")])
                    tV[(j, "mr2")] = sV.inc(
                        v.tensor_copy(s_mr2[j % 2][:, :], psb[0:2, 2048:2176]))

            def emit_workB(j):
                c1 = j // CH1
                r1 = (j % CH1) * P
                c2 = j // CH2
                x2b = s_x2[0]
                afb = s_af[0]
                amb = s_am[j % 2]
                xhb = s_xh[0]
                m2b = s_m2[j % 2]
                mlb = s_mlb[j % 2]
                frb = s_fr[j % FRING]
                s1b = s_s1[j % FRING]
                moa = s_mo4a[j % 4]
                mosb = s_mo4b[j % 4]
                pf = ps2[:, (j % 2) * H:(j % 2 + 1) * H]
                st = s_st2[j % 2]
                mc2 = s_mc2[j % 2]
                mcb = s_mcb[j % 2]

                @block.tensor
                def _(t, j=j, pf=pf):
                    W(t, tV[(j, "yTe")])
                    for k in range(KC):
                        W(t, wfa_tok[k])
                    W(t, exf_row[0])
                    if j >= 2:
                        W(t, tV[(j - 2, "fev")])
                    for k in range(KC):
                        for n in range(2):
                            t.matmul(
                                pf[:, n * 512:(n + 1) * 512],
                                s_yT[j % 2][:, k * P:(k + 1) * P],
                                s_wf[:, k * H + n * 512:k * H + (n + 1) * 512],
                                start=(k == 0), stop=False)
                    W(t, tV[(j, "mr2")])
                    ins = None
                    for n in range(2):
                        ins = t.matmul(pf[:, n * 512:(n + 1) * 512], s_mr2[j % 2][:, :],
                                       s_ex["f"][:, n * 512:(n + 1) * 512],
                                       start=False, stop=True)
                    tT[(j, "f")] = sT.inc(ins)

                @block.vector
                def _(v, j=j, frb=frb, pf=pf, st=st, mosb=mosb, moa=moa):
                    W(v, tT[(j, "f")])
                    if j >= FRING:
                        W(v, tV[(j - FRING, "ot")])
                    tV[(j, "fev")] = vt(v, v.tensor_scalar(
                        frb[:, :], pf[:, :], st[:, 21:22], None, ALU.mult))
                    if j >= 4:
                        W(v, tD[(j - 4, "mo4a")])
                    tV[(j, "fred")] = sV.inc(
                        v.reduce_sum(moa[:, 0:1], frb[:, :], axis=AX))
                    if j >= 4:
                        W(v, tD[(j - 4, "mo4b")])
                    tV[(j, "fsq")] = sV.inc(
                        v.scalar_tensor_tensor(s_dmp[:, :], frb[:, :], 1.0,
                                               frb[:, :], ALU.mult, ALU.mult,
                                               accum_out=mosb[:, 0:1]))

                @block.gpsimd
                def _(g, j=j, frb=frb, moa=moa, xhb=xhb, s1b=s1b, c2=c2,
                      mosb=mosb):
                    r2 = (j % CH2) * P
                    W(g, tV[(j, "fred")])
                    tD[(j, "mo4a")] = dma(g, dmo4[j % 4],
                                          m4sp[c2].ap()[r2:r2 + P, :], moa[:, :])
                    W(g, tV[(j, "fsq")])
                    tD[(j, "mo4b")] = dma(g, dmo4b[j % 4],
                                          m4sp[c2].ap()[RPC2 + r2:RPC2 + r2 + P, :],
                                          mosb[:, :])
                    if j % CH2 == CH2 - 1:
                        for b in range(4):
                            W(g, tD[(j - ((j - b) % 4), "mo4b")])
                            W(g, tD[(j - ((j - b) % 4), "mo4a")])
                        cc4.inc(g.collective_compute(
                            "AllGather", ALU.bypass, replica_groups=RG,
                            ins=[m4sp[c2].ap().opt()], outs=[m4gt[c2].ap().opt()]))

            def emit_finalize_chunk(c2):
                jfs = list(range(c2 * CH2, (c2 + 1) * CH2))

                @block.gpsimd
                def _(g, c2=c2, jfs=jfs):
                    W(g, (cc4, c2 + 1))
                    for jf in jfs:
                        r2 = (jf % CH2) * P
                        m4b = s_m4[jf % 4]
                        if jf >= 4:
                            W(g, tV[(jf - 4, "r4")])
                        dma(g, dm4[jf % 4], m4b[:, 0:1],
                            m4gt[c2].ap()[r2:r2 + P, :])
                        dma(g, dm4[jf % 4], m4b[:, 1:2],
                            m4gt[c2].ap()[RPC2 + r2:RPC2 + r2 + P, :])
                        dma(g, dm4[jf % 4], m4b[:, 2:3],
                            m4gt[c2].ap()[2 * RPC2 + r2:2 * RPC2 + r2 + P, :])
                        tD[(jf, "m4")] = dma(
                            g, dm4[jf % 4], m4b[:, 3:4],
                            m4gt[c2].ap()[3 * RPC2 + r2:3 * RPC2 + r2 + P, :])

                @block.vector
                def _(v, jfs=jfs):
                    for jf in jfs:
                        m4b = s_m4[jf % 4]
                        W(v, tD[(jf, "m4")])
                        vt(v, v.tensor_tensor(m4b[:, 4:6], m4b[:, 0:2],
                                              m4b[:, 2:4], ALU.add))
                        vt(v, v.tensor_scalar(m4b[:, 6:7], m4b[:, 4:5], 1.0 / D,
                                              None, ALU.mult))
                        vt(v, v.tensor_tensor(m4b[:, 7:8], m4b[:, 6:7],
                                              m4b[:, 6:7], ALU.mult))
                        tV[(jf, "var4")] = sV.inc(
                            v.tensor_scalar(m4b[:, 8:9], m4b[:, 5:6], 1.0 / D,
                                            m4b[:, 7:8], ALU.mult, ALU.subtract))

                @block.scalar
                def _(s, jfs=jfs):
                    for jf in jfs:
                        m4b = s_m4[jf % 4]
                        W(s, tV[(jf, "var4")])
                        tS[(jf, "std4")] = sS.inc(
                            s.activation(m4b[:, 11:12], m4b[:, 8:9], AF.Sqrt,
                                         bias=s_eps[:, :]))

                @block.vector
                def _(v, jfs=jfs):
                    for jf in jfs:
                        m4b = s_m4[jf % 4]
                        frb = s_fr[jf % FRING]
                        s1b = s_s1[jf % FRING]
                        otb = s_ot[jf % 4]
                        W(v, tS[(jf, "std4")])
                        vt(v, v.reciprocal(m4b[:, 9:10], m4b[:, 11:12]))
                        tV[(jf, "r4")] = vt(v, v.tensor_scalar(
                            m4b[:, 10:11], m4b[:, 6:7], m4b[:, 9:10], -1.0,
                            ALU.mult, ALU.mult))
                        vt(v, v.scalar_tensor_tensor(
                            s1b[:, :], frb[:, :], m4b[:, 9:10], s1b[:, :],
                            ALU.mult, ALU.add))
                        if jf >= 4:
                            W(v, tD[(jf - 4, "od")])
                        tV[(jf, "ot")] = vt(v, v.tensor_scalar(
                            otb[:, :], s1b[:, :], m4b[:, 10:11], None, ALU.add))

                @block.gpsimd
                def _(g, jfs=jfs):
                    for jf in jfs:
                        otb = s_ot[jf % 4]
                        W(g, tV[(jf, "ot")])
                        tD[(jf, "od")] = dma(g, dso[jf % 4],
                                             out_d[jf * P:(jf + 1) * P, :],
                                             otb[:, :])

            emit_stats(0)
            for j in range(NT):
                if j % CH2 == 0 and j // CH2 >= 2:
                    emit_finalize_chunk(j // CH2 - 2)
                emit_workA(j)
                if j + 1 < NT:
                    emit_stats(j + 1)
                if j >= 1:
                    emit_workB(j - 1)
            emit_workB(NT - 1)
            emit_finalize_chunk(NCH2 - 2)
            emit_finalize_chunk(NCH2 - 1)

            @block.gpsimd
            def _(g):
                for b in range(4):
                    g.wait_ge(dso[b].h, dso[b].n)

    return nc


def prep_inputs(xb, Wq, bq, Wk, bk, Wv, bv, Wg, bg, Wf, bf, pi):
    f32 = np.float32
    lo, hi = pi * H, (pi + 1) * H
    return {
        "xT": np.ascontiguousarray(xb.T).astype(f32, copy=False),
        "x": np.ascontiguousarray(xb),
        "xh": np.ascontiguousarray(xb[:, lo:hi]),
        "wq": np.ascontiguousarray(Wq.T[:, lo:hi]),
        "wk": np.ascontiguousarray(Wk.T[:, lo:hi]),
        "wv": np.ascontiguousarray(Wv.T[:, lo:hi]),
        "wg": np.ascontiguousarray(Wg.T[:, lo:hi]),
        "wf": np.ascontiguousarray(Wf.T[:, lo:hi]),
        "bq": np.ascontiguousarray(bq[lo:hi]).reshape(1, H),
        "bk": np.ascontiguousarray(bk[lo:hi]).reshape(1, H),
        "bv": np.ascontiguousarray(bv[lo:hi]).reshape(1, H),
        "bg": np.ascontiguousarray(bg[lo:hi]).reshape(1, H),
        "bf": np.ascontiguousarray(bf[lo:hi]).reshape(1, H),
        "tri": np.triu(np.ones((P, P), dtype=f32)),
        "id32": np.eye(P, dtype=f32),
        "idbf": np.eye(P, dtype=ml_dtypes.bfloat16),
        "onc": np.ones((P, 1), dtype=ml_dtypes.bfloat16),
        "onr": np.ones((1, P), dtype=f32),
    }


def kernel(L0_attn_x, L0_attn_Wq, L0_attn_bq, L0_attn_Wk, L0_attn_bk,
           L0_attn_Wv, L0_attn_bv, L0_attn_Wg, L0_attn_bg,
           L0_ffn_weight, L0_ffn_bias):
    x = np.asarray(L0_attn_x, dtype=np.float32)
    B, SEQ, Dm = x.shape
    assert Dm == D and B * 2 == 8
    nc = build_nc(SEQ)
    in_maps = []
    for c in range(8):
        p, pi = c // 2, c % 2
        in_maps.append(prep_inputs(
            x[p], np.asarray(L0_attn_Wq), np.asarray(L0_attn_bq),
            np.asarray(L0_attn_Wk), np.asarray(L0_attn_bk),
            np.asarray(L0_attn_Wv), np.asarray(L0_attn_bv),
            np.asarray(L0_attn_Wg), np.asarray(L0_attn_bg),
            np.asarray(L0_ffn_weight), np.asarray(L0_ffn_bias), pi))
    res = run_bass_kernel_spmd(nc, in_maps, core_ids=list(range(8)))
    out = np.empty((B, SEQ, D), dtype=np.float32)
    for c in range(8):
        p, pi = c // 2, c % 2
        out[p][:, pi * H:(pi + 1) * H] = res.results[c]["out"]
    return out
